# version 20 (final)
# baseline (speedup 1.0000x reference)
import numpy as np
from contextlib import ExitStack

import concourse.bass as bass
import concourse.mybir as mybir
from concourse.bass_utils import run_bass_kernel_spmd

F32 = mybir.dt.float32
BF16 = mybir.dt.bfloat16
I32 = mybir.dt.int32
AF = mybir.ActivationFunctionType
ALU = mybir.AluOpType
AX = mybir.AxisListType

T = 2048
D = 1024
DC = D // 128
NG = T // 512
S = 64
H0C = 3072 // 128
H1C = 4096 // 128
NH, DH = 16, 64
LN_EPS = 1e-5
RMS_EPS = 1e-6
QS = 1.0 / 127.0

_NMC = [0]


def _nm():
    _NMC[0] += 1
    return f"t{_NMC[0]}"


def _prep_w(W):
    W = np.ascontiguousarray(W)
    Dout, Din = W.shape
    mb = 128 if Dout % 128 == 0 else Dout
    kb = 128 if Din % 128 == 0 else Din
    Mc, Kc = Dout // mb, Din // kb
    H = W.reshape(Mc, mb, Kc, kb).transpose(0, 3, 2, 1)
    return np.ascontiguousarray(H.reshape(Mc, kb, Kc * mb))


def _prep_pc(v):
    n = v.shape[0]
    if n >= 128:
        return np.ascontiguousarray(v.reshape(n // 128, 128).T)
    return np.ascontiguousarray(v.reshape(n, 1))


def _fm(x):
    t, d = x.shape
    return np.ascontiguousarray(
        x.reshape(t, d // 128, 128).transpose(2, 1, 0).reshape(128, (d // 128) * t))


class Sem:
    def __init__(self, h):
        self.h = h
        self.base = 0
        self.cnt = 0

    def w(self, eng, rel):
        if rel > 0:
            eng.wait_ge(self.h, self.base + rel)

    def inc(self, ins, n=1):
        ins.then_inc(self.h, n)
        self.cnt += n
        return ins


class K:
    def __init__(self, nc, stack):
        self.nc = nc
        self.st = stack
        self._free = []
        self._nsem = 0
        import ml_dtypes
        bf = ml_dtypes.bfloat16
        self.i_identb = nc.inline_tensor(np.eye(128, dtype=bf), "c_identb")
        self.i_identf = nc.inline_tensor(np.eye(128, dtype=np.float32), "c_identf")
        self.i_onesc = nc.inline_tensor(np.ones((128, 1), dtype=bf), "c_onesc")
        self.i_onesr = nc.inline_tensor(np.ones((1, 128), dtype=bf), "c_onesr")
        mask = np.zeros((128, 16 * 32), dtype=bf)
        sel = np.zeros((32, 16 * 128), dtype=bf)
        for c in range(16):
            mask[0:64, c * 32 + 2 * c] = 1.0
            mask[64:128, c * 32 + 2 * c + 1] = 1.0
            sel[2 * c, c * 128:c * 128 + 64] = 1.0
            sel[2 * c + 1, c * 128 + 64:c * 128 + 128] = 1.0
        self.i_mask16 = nc.inline_tensor(mask, "c_mask16")
        self.i_sel16 = nc.inline_tensor(sel, "c_sel16")
        e = stack.enter_context
        self.identb = e(nc.sbuf_tensor("cs_identb", [128, 128], BF16))
        self.identf = e(nc.sbuf_tensor("cs_identf", [128, 128], F32))
        self.onesc = e(nc.sbuf_tensor("cs_onesc", [128, 1], BF16))
        self.onesr = e(nc.sbuf_tensor("cs_onesr", [1, 128], BF16))
        self.mask16 = e(nc.sbuf_tensor("cs_mask16", [128, 16 * 32], BF16))
        self.sel16 = e(nc.sbuf_tensor("cs_sel16", [32, 16 * 128], BF16))
        self.pad_sb = e(nc.sbuf_tensor("cs_pad", [128, 1024], F32))

    def delay(self, eng, n=4):
        for i in range(n):
            a, b = (0, 512) if i % 2 == 0 else (512, 0)
            eng.tensor_copy(self.pad_sb[:, a:a + 512],
                            self.pad_sb[:, b:b + 512])

    def delay_act(self, eng, n=4):
        for i in range(n):
            a, b = (0, 512) if i % 2 == 0 else (512, 0)
            eng.copy(self.pad_sb[:, a:a + 512], self.pad_sb[:, b:b + 512])

    def sems(self, n, dma=False):
        out = []
        for _ in range(n):
            if self._free and not dma:
                out.append(self._free.pop())
            else:
                self._nsem += 1
                h = self.st.enter_context(self.nc.semaphore(f"s{self._nsem}"))
                s = Sem(h)
                s.dma = dma
                out.append(s)
        return out

    def rel(self, *sems):
        for s in sems:
            s.base += s.cnt
            s.cnt = 0
            if not getattr(s, "dma", False):
                self._free.append(s)

    def load_consts(self):
        nc = self.nc
        (s,) = self.sems(1, dma=True)
        with nc.Block() as blk:
            @blk.sync
            def _(q):
                for dst, src in ((self.identb, self.i_identb),
                                 (self.identf, self.i_identf),
                                 (self.onesc, self.i_onesc),
                                 (self.onesr, self.i_onesr),
                                 (self.mask16, self.i_mask16),
                                 (self.sel16, self.i_sel16)):
                    s.inc(q.dma_start(dst[:], src.ap()), 16)
                s.w(q, 96)

            @blk.vector
            def _(v):
                v.memset(self.pad_sb[:], 0.0)
        self.rel(s)


def _wload(q, dst_sb, w_ap):
    Mc, kb, X = w_ap.shape
    dst3 = dst_sb[:].rearrange("p (m x) -> p m x", x=X)
    return q.dma_start(dst3, w_ap.rearrange("m p x -> p m x"))


def emit_ln(k, x_hbm, xn, gamma_col=None, beta_col=None, vecs=None, h_src=None):
    nc = k.nc
    with ExitStack() as sc:
        e = sc.enter_context
        xb = e(nc.sbuf_tensor(_nm(), [128, DC * T], BF16)) if h_src is None else h_src
        xsq = e(nc.sbuf_tensor(_nm(), [128, DC * 512], BF16))
        rows = e(nc.sbuf_tensor(_nm(), [1, 3 * T], F32))
        mu = rows[:, 0:T]
        e2 = rows[:, T:2 * T]
        t3 = rows[:, 2 * T:3 * T]
        rrow = e(nc.sbuf_tensor(_nm(), [1, T], BF16))
        mrow = e(nc.sbuf_tensor(_nm(), [1, T], BF16))
        rb = e(nc.sbuf_tensor(_nm(), [128, T], BF16))
        mb_ = e(nc.sbuf_tensor(_nm(), [128, T], BF16))
        s_ld, s_ld2 = k.sems(2, dma=True)
        s_sq, s_stat, s_sev, s_rows, s_bc, s_bev = k.sems(6)

        with ExitStack() as psc:
            pe = psc.enter_context
            ps_sum = pe(nc.psum_tensor(_nm(), [1, 512], F32))
            ps_sq = pe(nc.psum_tensor(_nm(), [1, 512], F32))
            ps_pad = pe(nc.psum_tensor(_nm(), [128, 512], F32))
            ps_r = [pe(nc.psum_tensor(_nm(), [128, 512], F32)) for _ in range(2)]
            ps_m = [pe(nc.psum_tensor(_nm(), [128, 512], F32)) for _ in range(2)]

            with nc.Block() as blk:
                if h_src is None:
                    xb3 = xb[:].rearrange("p (c t) -> p c t", t=T)
                    x3 = x_hbm.rearrange("p (c t) -> p c t", t=T)

                    @blk.gpsimd
                    def _(q):
                        s_ld.inc(q.dma_start(xb3[:, :, 0:1024],
                                             x3[:, :, 0:1024]), 16)
                        s_ld2.inc(q.dma_start(xb3[:, :, 1024:2048],
                                              x3[:, :, 1024:2048]), 16)

                @blk.scalar
                def _(scl):
                    for g in range(NG):
                        if h_src is None:
                            s_ld.w(scl, 16)
                            if g >= 2:
                                s_ld2.w(scl, 16)
                        if g > 0:
                            s_stat.w(scl, 2 * g)
                        for c in range(DC):
                            s_sq.inc(scl.activation(
                                xsq[:, c * 512:(c + 1) * 512],
                                xb[:, c * T + g * 512: c * T + g * 512 + 512],
                                AF.Square))

                @blk.tensor
                def _(t):
                    for g in range(NG):
                        if h_src is None:
                            s_ld.w(t, 16)
                            if g >= 2:
                                s_ld2.w(t, 16)
                        if g > 0:
                            s_sev.w(t, 2 * g)
                        for c in range(DC):
                            ins = t.matmul(
                                ps_sum[:, 0:512], k.onesc[:],
                                xb[:, c * T + g * 512: c * T + g * 512 + 512],
                                start=(c == 0), stop=(c == DC - 1))
                        s_stat.inc(ins)
                        for c in range(DC):
                            s_sq.w(t, g * DC + c + 1)
                            ins = t.matmul(
                                ps_sq[:, 0:512], k.onesc[:],
                                xsq[:, c * 512:(c + 1) * 512],
                                start=(c == 0), stop=(c == DC - 1))
                        s_stat.inc(ins)
                    for _ in range(4):
                        s_stat.inc(t.matmul(ps_pad[:, 0:512], k.identb[:],
                                            xb[:, 0:512], start=True, stop=True))

                @blk.vector
                def _(v):
                    for g in range(NG):
                        s_stat.w(v, 2 * g + 2)
                        v.tensor_copy(mu[:, g * 512:(g + 1) * 512],
                                      ps_sum[:, 0:512])
                        s_sev.inc(v.tensor_copy(e2[:, g * 512:(g + 1) * 512],
                                                ps_sq[:, 0:512]), 2)
                    v.tensor_scalar_mul(mu, mu, 1.0 / D)
                    v.tensor_scalar(e2, e2, 1.0 / D, float(LN_EPS),
                                    ALU.mult, ALU.add)
                    v.tensor_mul(t3, mu, mu)
                    s_rows.inc(v.tensor_sub(e2, e2, t3))

            with nc.Block() as blk:
                @blk.scalar
                def _(scl):
                    s_rows.w(scl, 1)
                    scl.activation(t3, e2, AF.Ln)
                    s_rows.inc(scl.activation(rrow[:], t3, AF.Exp, scale=-0.5),
                               1)

                @blk.vector
                def _(v):
                    s_rows.w(v, 2)
                    v.tensor_copy(e2, rrow[:])
                    v.tensor_mul(t3, mu, e2)
                    s_rows.inc(v.tensor_scalar_mul(mrow[:], t3, -1.0))

            with nc.Block() as blk:
                @blk.tensor
                def _(t):
                    s_rows.w(t, 3)
                    for g in range(NG):
                        if g >= 2:
                            s_bev.w(t, 2 * (g - 1))
                        s_bc.inc(t.matmul(ps_r[g % 2][:, 0:512], k.onesr[:],
                                          rrow[:, g * 512:(g + 1) * 512],
                                          start=True, stop=True))
                        s_bc.inc(t.matmul(ps_m[g % 2][:, 0:512], k.onesr[:],
                                          mrow[:, g * 512:(g + 1) * 512],
                                          start=True, stop=True))

                @blk.scalar
                def _(scl):
                    for g in range(NG):
                        s_bc.w(scl, 2 * g + 2)
                        scl.copy(rb[:, g * 512:(g + 1) * 512], ps_r[g % 2][:, 0:512])
                        s_bev.inc(scl.copy(mb_[:, g * 512:(g + 1) * 512],
                                           ps_m[g % 2][:, 0:512]), 2)

                @blk.vector
                def _(v):
                    for g in range(NG):
                        s_bev.w(v, 2 * (g + 1))
                        for c in range(DC):
                            sl = slice(c * T + g * 512, c * T + g * 512 + 512)
                            gl = slice(g * 512, g * 512 + 512)
                            v.tensor_mul(xn[:, sl], xb[:, sl], rb[:, gl])
                            v.tensor_add(xn[:, sl], xn[:, sl], mb_[:, gl])
                            if gamma_col is not None:
                                v.tensor_scalar(
                                    xn[:, sl], xn[:, sl],
                                    vecs[:, gamma_col + c:gamma_col + c + 1],
                                    vecs[:, beta_col + c:beta_col + c + 1],
                                    ALU.mult, ALU.add)
        k.rel(s_ld, s_ld2, s_sq, s_stat, s_sev, s_rows, s_bc, s_bev)


class Proj:
    def __init__(self, k, scope, psc, w_hbm, Mc, Kc, rhs_fn, ng=NG):
        self.k, self.w = k, w_hbm
        self.Mc, self.Kc, self.rhs_fn, self.ng = Mc, Kc, rhs_fn, ng
        nc = k.nc
        self.wbuf = [scope.enter_context(nc.sbuf_tensor(_nm(), [128, Kc * 128], BF16))
                     for _ in range(2)]
        self.ps = [psc.enter_context(nc.psum_tensor(_nm(), [128, 512], F32))
                   for _ in range(3)]
        (self.s_w,) = k.sems(1, dma=True)
        self.s_mm, self.s_ev = k.sems(2)

    def emit_dma(self, q):
        for mc in range(self.Mc):
            if mc >= 2:
                self.s_ev.w(q, (mc - 1) * self.ng)
            self.s_w.inc(q.dma_start(self.wbuf[mc % 2][:], self.w[mc]), 16)

    def emit_pe(self, t, pad_ps):
        for mc in range(self.Mc):
            self.s_w.w(t, 16 * (mc + 1))
            for g in range(self.ng):
                idx = mc * self.ng + g
                if idx >= 3:
                    self.s_ev.w(t, idx - 2)
                for kc in range(self.Kc):
                    ins = t.matmul(
                        self.ps[idx % 3][:, 0:512],
                        self.wbuf[mc % 2][:, kc * 128:(kc + 1) * 128],
                        self.rhs_fn(kc, g), start=(kc == 0),
                        stop=(kc == self.Kc - 1))
                self.s_mm.inc(ins)
        for _ in range(4):
            self.s_mm.inc(t.matmul(pad_ps[:, 0:512], self.k.identb[:],
                                   self.wbuf[0][:, 0:512], start=True, stop=True))

    def emit_evac(self, eng, evac_fn):
        for mc in range(self.Mc):
            for g in range(self.ng):
                idx = mc * self.ng + g
                self.s_mm.w(eng, idx + 1)
                self.s_ev.inc(evac_fn(eng, mc, g, self.ps[idx % 3][:, 0:512]))

    def done(self):
        self.k.rel(self.s_w, self.s_mm, self.s_ev)


def emit_resid_proj(k, src_fn, w_hbm, Mc, Kc, res_hbm, out_hbm, vecs,
                    bias_col=None, scale_col=None, scale_imm=None):
    nc = k.nc
    with ExitStack() as sc, ExitStack() as psc:
        e = sc.enter_context
        xres = e(nc.sbuf_tensor(_nm(), [128, 2 * T], F32))
        x1 = e(nc.sbuf_tensor(_nm(), [128, 2 * T], F32))
        s_res, s_st = k.sems(2, dma=True)
        pr = Proj(k, sc, psc, w_hbm, Mc, Kc, src_fn)
        ps_pad = psc.enter_context(nc.psum_tensor(_nm(), [128, 512], F32))

        with nc.Block() as blk:
            @blk.gpsimd
            def _(q):
                pr.emit_dma(q)

            @blk.sync
            def _(q):
                for mc in range(Mc):
                    if mc >= 2:
                        pr.s_ev.w(q, (mc - 1) * NG)
                        s_st.inc(q.dma_start(
                            out_hbm[:, (mc - 2) * T:(mc - 1) * T],
                            x1[:, (mc % 2) * T:(mc % 2) * T + T]), 16)
                    s_res.inc(q.dma_start(
                        xres[:, (mc % 2) * T:(mc % 2) * T + T],
                        res_hbm[:, mc * T:(mc + 1) * T]), 16)
                for mc in (Mc - 2, Mc - 1):
                    pr.s_ev.w(q, (mc + 1) * NG)
                    s_st.inc(q.dma_start(
                        out_hbm[:, mc * T:(mc + 1) * T],
                        x1[:, (mc % 2) * T:(mc % 2) * T + T]), 16)
                s_st.w(q, 16 * Mc)

            @blk.tensor
            def _(t):
                pr.emit_pe(t, ps_pad)

            @blk.vector
            def _(v):
                def evac(eng, mc, g, ps):
                    s_res.w(eng, 16 * (mc + 1))
                    if mc >= 2 and g == 0:
                        s_st.w(eng, 16 * (mc - 1))
                    sl = slice((mc % 2) * T + g * 512,
                               (mc % 2) * T + g * 512 + 512)
                    if bias_col is not None:
                        return eng.scalar_tensor_tensor(
                            x1[:, sl], ps,
                            vecs[:, bias_col + mc:bias_col + mc + 1],
                            xres[:, sl], ALU.add, ALU.add)
                    if scale_col is not None:
                        return eng.scalar_tensor_tensor(
                            x1[:, sl], ps,
                            vecs[:, scale_col + mc:scale_col + mc + 1],
                            xres[:, sl], ALU.mult, ALU.add)
                    return eng.scalar_tensor_tensor(
                        x1[:, sl], ps, float(scale_imm), xres[:, sl],
                        ALU.mult, ALU.add)
                pr.emit_evac(v, evac)
        pr.done()
        k.rel(s_res, s_st)


def emit_l0attn(k, x_hbm, y_hbm, W):
    nc = k.nc
    with ExitStack() as sc:
        e = sc.enter_context
        vecs = e(nc.sbuf_tensor(_nm(), [128, 48], F32))
        xn = e(nc.sbuf_tensor(_nm(), [128, DC * T], BF16))
        hn = e(nc.sbuf_tensor(_nm(), [128, DC * T], BF16))
        (s_v,) = k.sems(1, dma=True)
        with nc.Block() as blk:
            @blk.gpsimd
            def _(q):
                s_v.inc(q.dma_start(vecs[:, 0:8], W["a_D"]), 16)
                s_v.inc(q.dma_start(vecs[:, 8:16], W["a_bout"]), 16)
                s_v.inc(q.dma_start(vecs[:, 16:24], W["a_gamma"]), 16)
                s_v.inc(q.dma_start(vecs[:, 24:32], W["a_beta"]), 16)
                s_v.inc(q.dma_start(vecs[0:64, 32:33], W["a_Alog"]), 16)
                s_v.inc(q.dma_start(vecs[0:64, 33:34], W["a_bg"]), 16)

        emit_ln(k, x_hbm, xn)

        with ExitStack() as sc2:
            e2 = sc2.enter_context
            wg_sb = e2(nc.sbuf_tensor(_nm(), [128, DC * S], BF16))
            wbx_sb = e2(nc.sbuf_tensor(_nm(), [128, DC * S], BF16))
            wc_sb = e2(nc.sbuf_tensor(_nm(), [64, DC * 128], BF16))
            gate_sb = e2(nc.sbuf_tensor(_nm(), [64, T], BF16))
            u_sb = e2(nc.sbuf_tensor(_nm(), [64, T], F32))
            ab_sb = e2(nc.sbuf_tensor(_nm(), [64, T], F32))
            states_b = e2(nc.sbuf_tensor(_nm(), [64, T], BF16))
            h = e2(nc.sbuf_tensor(_nm(), [128, DC * T], BF16))
            (s_w2,) = k.sems(1, dma=True)
            s_gb, s_gev, s_u, s_scan, s_ymm, s_yev, s_aexp = k.sems(7)

            with ExitStack() as psc, nc.Block() as blk:
                pe = psc.enter_context
                ps_g = [pe(nc.psum_tensor(_nm(), [64, 512], F32)) for _ in range(2)]
                ps_b = [pe(nc.psum_tensor(_nm(), [64, 512], F32)) for _ in range(2)]
                ps_pad = pe(nc.psum_tensor(_nm(), [128, 512], F32))

                @blk.gpsimd
                def _(q):
                    s_w2.inc(_wload(q, wg_sb, W["a_wg"]), 16)
                    s_w2.inc(_wload(q, wbx_sb, W["a_wbx"]), 16)
                    s_w2.inc(_wload(q, wc_sb, W["a_wc"]), 16)

                @blk.scalar
                def _(scl):
                    s_v.w(scl, 96)
                    s_aexp.inc(scl.activation(vecs[0:64, 32:33],
                                              vecs[0:64, 32:33], AF.Exp))
                    for g in range(NG):
                        s_gb.w(scl, 2 * g + 1)
                        s_gev.inc(scl.activation(
                            gate_sb[:, g * 512:(g + 1) * 512],
                            ps_g[g % 2][0:64, 0:512], AF.Sigmoid,
                            bias=vecs[0:64, 33:34]))

                @blk.tensor
                def _(t):
                    s_w2.w(t, 32)
                    for g in range(NG):
                        if g >= 2:
                            s_u.w(t, g - 1)
                        for kc in range(DC):
                            ins = t.matmul(
                                ps_g[g % 2][0:64, 0:512],
                                wg_sb[:, kc * S:(kc + 1) * S],
                                xn[:, kc * T + g * 512: kc * T + g * 512 + 512],
                                start=(kc == 0), stop=(kc == DC - 1))
                        s_gb.inc(ins)
                        for kc in range(DC):
                            ins = t.matmul(
                                ps_b[g % 2][0:64, 0:512],
                                wbx_sb[:, kc * S:(kc + 1) * S],
                                xn[:, kc * T + g * 512: kc * T + g * 512 + 512],
                                start=(kc == 0), stop=(kc == DC - 1))
                        s_gb.inc(ins)
                    for _ in range(4):
                        s_gb.inc(t.matmul(ps_pad[:, 0:512], k.identb[:],
                                          xn[:, 0:512], start=True, stop=True))

                @blk.vector
                def _(v):
                    s_v.w(v, 96)
                    s_aexp.w(v, 1)
                    v.tensor_scalar(ab_sb[0:64, :], xn[0:64, 0:T], 0.0,
                                    vecs[0:64, 32:33], ALU.mult, ALU.add)
                    v.tensor_scalar_add(vecs[:, 34:42], vecs[:, 0:8], 1.0)
                    for g in range(NG):
                        s_gev.w(v, g + 1)
                        s_gb.w(v, 2 * g + 2)
                        s_u.inc(v.tensor_mul(
                            u_sb[0:64, g * 512:(g + 1) * 512],
                            gate_sb[0:64, g * 512:(g + 1) * 512],
                            ps_b[g % 2][0:64, 0:512]))
                    s_scan.inc(v.tensor_tensor_scan(
                        states_b[0:64, :], ab_sb[0:64, :], u_sb[0:64, :],
                        0.0, ALU.mult, ALU.add))

            with ExitStack() as psc, nc.Block() as blk:
                pe = psc.enter_context
                ps_y = [pe(nc.psum_tensor(_nm(), [128, 512], F32)) for _ in range(3)]
                ps_pad = pe(nc.psum_tensor(_nm(), [128, 512], F32))

                @blk.tensor
                def _(t):
                    s_scan.w(t, 1)
                    s_w2.w(t, 48)
                    for mc in range(DC):
                        for g in range(NG):
                            idx = mc * NG + g
                            if idx >= 3:
                                s_yev.w(t, idx - 2)
                            s_ymm.inc(t.matmul(
                                ps_y[idx % 3][:, 0:512],
                                wc_sb[:, mc * 128:(mc + 1) * 128],
                                states_b[0:64, g * 512:(g + 1) * 512],
                                start=True, stop=True))
                    for _ in range(4):
                        s_ymm.inc(t.matmul(ps_pad[:, 0:512], k.identb[:],
                                           xn[:, 0:512], start=True, stop=True))

                @blk.vector
                def _(v):
                    for mc in range(DC):
                        for g in range(NG):
                            idx = mc * NG + g
                            s_ymm.w(v, idx + 1)
                            sl = slice(mc * T + g * 512, mc * T + g * 512 + 512)
                            s_yev.inc(v.scalar_tensor_tensor(
                                h[:, sl], xn[:, sl], vecs[:, 34 + mc:35 + mc],
                                ps_y[idx % 3][:, 0:512], ALU.mult, ALU.add))

            k.rel(s_w2, s_gb, s_gev, s_u, s_scan, s_ymm, s_yev, s_aexp)
            emit_ln(k, None, hn, gamma_col=16, beta_col=24, vecs=vecs, h_src=h)

            if globals().get("_DBG"):
                dbg = {"d_ab": (ab_sb, F32, [64, T]),
                       "d_gate": (gate_sb, BF16, [64, T]),
                       "d_u": (u_sb, F32, [64, T]),
                       "d_st": (states_b, BF16, [64, T]),
                       "d_h": (h, BF16, [128, DC * T]),
                       "d_hn": (hn, BF16, [128, DC * T]),
                       "d_xn": (xn, BF16, [128, DC * T])}
                (sdb,) = k.sems(1, dma=True)
                outs = {n: nc.dram_tensor(n, shp, dt, kind="ExternalOutput").ap()
                        for n, (_, dt, shp) in dbg.items()}
                with nc.Block() as blk:
                    @blk.gpsimd
                    def _(q):
                        for n, (tile, _, _2) in dbg.items():
                            sdb.inc(q.dma_start(outs[n], tile[:]), 16)
                        sdb.w(q, 16 * len(dbg))
                k.rel(sdb)

        emit_resid_proj(
            k, lambda kc, g: hn[:, kc * T + g * 512: kc * T + g * 512 + 512],
            W["a_wout"], DC, DC, x_hbm, y_hbm, vecs, bias_col=8)
        k.rel(s_v)


def emit_glu(k, sc, psc, w1_hbm, w2_hbm, Mc, Kc, hout, hT, rhs_fn, ng=NG,
             silu_scale_imm=None, s1_col=None, s2_col=None, vecs=None):
    nc = k.nc
    e = sc.enter_context
    wb1 = [e(nc.sbuf_tensor(_nm(), [128, Kc * 128], BF16)) for _ in range(2)]
    wb2 = [e(nc.sbuf_tensor(_nm(), [128, Kc * 128], BF16)) for _ in range(2)]
    sg = e(nc.sbuf_tensor(_nm(), [128, 2 * 512], BF16))
    ug = e(nc.sbuf_tensor(_nm(), [128, 2 * 512], BF16))
    pe = psc.enter_context
    ps_g = [pe(nc.psum_tensor(_nm(), [128, 512], F32)) for _ in range(2)]
    ps_u = [pe(nc.psum_tensor(_nm(), [128, 512], F32)) for _ in range(2)]
    ps_pad = pe(nc.psum_tensor(_nm(), [128, 512], F32))
    (s_w,) = k.sems(1, dma=True)
    s_mm, s_sev, s_hev = k.sems(3)
    ntot = Mc * ng

    with nc.Block() as blk:
        @blk.gpsimd
        def _(q):
            for mc in range(Mc):
                if mc >= 2:
                    s_hev.w(q, (mc - 1) * ng)
                s_w.inc(q.dma_start(wb1[mc % 2][:], w1_hbm[mc]), 16)
                s_w.inc(q.dma_start(wb2[mc % 2][:], w2_hbm[mc]), 16)

        @blk.tensor
        def _(t):
            for mc in range(Mc):
                s_w.w(t, 32 * (mc + 1))
                for g in range(ng):
                    idx = mc * ng + g
                    if idx >= 2:
                        s_hev.w(t, idx - 1)
                    for kc in range(Kc):
                        ins = t.matmul(ps_g[idx % 2][:, 0:512],
                                       wb1[mc % 2][:, kc * 128:(kc + 1) * 128],
                                       rhs_fn(kc, g), start=(kc == 0),
                                       stop=(kc == Kc - 1))
                    s_mm.inc(ins)
                    for kc in range(Kc):
                        ins = t.matmul(ps_u[idx % 2][:, 0:512],
                                       wb2[mc % 2][:, kc * 128:(kc + 1) * 128],
                                       rhs_fn(kc, g), start=(kc == 0),
                                       stop=(kc == Kc - 1))
                    s_mm.inc(ins)
            for _ in range(4):
                s_mm.inc(t.matmul(ps_pad[:, 0:512], k.identb[:],
                                  wb1[0][:, 0:512], start=True, stop=True))

        @blk.scalar
        def _(scl):
            for mc in range(Mc):
                for g in range(ng):
                    idx = mc * ng + g
                    s_mm.w(scl, 2 * idx + 2)
                    if idx >= 2:
                        s_hev.w(scl, idx - 1)
                    so = slice((idx % 2) * 512, (idx % 2) * 512 + 512)
                    if silu_scale_imm is not None:
                        scl.activation(sg[:, so], ps_g[idx % 2][:, 0:512],
                                       AF.Silu, scale=float(silu_scale_imm))
                        i2 = scl.copy(ug[:, so], ps_u[idx % 2][:, 0:512])
                    else:
                        scl.activation(sg[:, so], ps_g[idx % 2][:, 0:512],
                                       AF.Silu,
                                       scale=vecs[:, s1_col + mc:s1_col + mc + 1])
                        i2 = scl.activation(ug[:, so], ps_u[idx % 2][:, 0:512],
                                            AF.Copy,
                                            scale=vecs[:, s2_col + mc:s2_col + mc + 1])
                    s_sev.inc(i2)

        @blk.vector
        def _(v):
            for mc in range(Mc):
                for g in range(ng):
                    idx = mc * ng + g
                    s_sev.w(v, idx + 1)
                    so = slice((idx % 2) * 512, (idx % 2) * 512 + 512)
                    s_hev.inc(v.tensor_mul(
                        hout[:, mc * hT + g * 512: mc * hT + g * 512 + 512],
                        sg[:, so], ug[:, so]))
    k.rel(s_w, s_mm, s_sev, s_hev)


def emit_l0ffn(k, x_hbm, y_hbm, W):
    nc = k.nc
    with ExitStack() as sc:
        e = sc.enter_context
        vecs = e(nc.sbuf_tensor(_nm(), [128, 1], F32))
        xn = e(nc.sbuf_tensor(_nm(), [128, DC * T], BF16))
        emit_ln(k, x_hbm, xn)
        h = e(nc.sbuf_tensor(_nm(), [128, H0C * T], BF16))
        with ExitStack() as sc2, ExitStack() as psc:
            emit_glu(k, sc2, psc, W["f_wgate"], W["f_wup"], H0C, DC, h, T,
                     lambda kc, g: xn[:, kc * T + g * 512: kc * T + g * 512 + 512],
                     silu_scale_imm=QS)
        emit_resid_proj(
            k, lambda kc, g: h[:, kc * T + g * 512: kc * T + g * 512 + 512],
            W["f_wdown"], DC, H0C, x_hbm, y_hbm, vecs, scale_imm=QS * QS)


def emit_l1attn(k, x_hbm, y_hbm, W):
    nc = k.nc
    with ExitStack() as sc:
        e = sc.enter_context
        vecs = e(nc.sbuf_tensor(_nm(), [128, 28], F32))
        z_tm = e(nc.sbuf_tensor(_nm(), [128, 16 * 1024], BF16))
        zsum = e(nc.sbuf_tensor(_nm(), [128, 256], F32))
        bv_sb = e(nc.sbuf_tensor(_nm(), [1, 1024], BF16))
        (s_v,) = k.sems(1, dma=True)
        with nc.Block() as blk:
            @blk.gpsimd
            def _(q):
                s_v.inc(q.dma_start(vecs[:, 0:8], W["c_bq"]), 16)
                s_v.inc(q.dma_start(vecs[:, 8:16], W["c_bk"]), 16)
                s_v.inc(q.dma_start(vecs[:, 16:17], W["c_wqg"]), 16)
                s_v.inc(q.dma_start(vecs[:, 17:18], W["c_wkg"]), 16)
                s_v.inc(q.dma_start(bv_sb[:], W["c_bv"]), 16)
                s_v.inc(q.dma_start(vecs[:, 20:28], W["c_bo"]), 16)

        with ExitStack() as scq:
            eq = scq.enter_context
            xn = eq(nc.sbuf_tensor(_nm(), [128, DC * T], BF16))
            emit_ln(k, x_hbm, xn)
            q_fm = eq(nc.sbuf_tensor(_nm(), [128, DC * T], BF16))
            k_fm = eq(nc.sbuf_tensor(_nm(), [128, DC * T], BF16))
            v_tm = eq(nc.sbuf_tensor(_nm(), [128, 16 * 1024], BF16))

            if True:
                with ExitStack() as sc3, ExitStack() as psc:
                    e3 = sc3.enter_context
                    wsb = [e3(nc.sbuf_tensor(_nm(), [128, DC * 1024], BF16))
                           for _ in range(2)]
                    pe = psc.enter_context
                    ps = [pe(nc.psum_tensor(_nm(), [128, 512], F32))
                          for _ in range(3)]
                    ps_pad = pe(nc.psum_tensor(_nm(), [128, 512], F32))
                    (s_w,) = k.sems(1, dma=True)
                    s_mm, s_ev = k.sems(2)
                    NSEC = DC * NG

                    with nc.Block() as blk:
                        @blk.gpsimd
                        def _(q):
                            for sec, wname in enumerate(("c_wq", "c_wk", "c_wv")):
                                if sec >= 2:
                                    s_ev.w(q, (sec - 1) * NSEC)
                                s_w.inc(_wload(q, wsb[sec % 2], W[wname]), 16)

                        @blk.tensor
                        def _(t):
                            s_v.w(t, 96)
                            idx = 0
                            for sec in range(3):
                                s_w.w(t, 16 * (sec + 1))
                                for a in range(DC if sec < 2 else 16):
                                    for b in range(NG if sec < 2 else 2):
                                        if idx >= 3:
                                            s_ev.w(t, idx - 2)
                                        p = ps[idx % 3]
                                        if sec < 2:
                                            for kc in range(DC):
                                                ins = t.matmul(
                                                    p[:, 0:512],
                                                    wsb[sec % 2][:, a * 1024 + kc * 128:
                                                                 a * 1024 + kc * 128 + 128],
                                                    xn[:, kc * T + b * 512:
                                                       kc * T + b * 512 + 512],
                                                    start=(kc == 0),
                                                    stop=(kc == DC - 1))
                                        else:
                                            for kc in range(DC):
                                                t.matmul(
                                                    p[:, 0:512],
                                                    xn[:, kc * T + a * 128:
                                                       kc * T + a * 128 + 128],
                                                    wsb[sec % 2][:, kc * 1024 + b * 512:
                                                                 kc * 1024 + b * 512 + 512],
                                                    start=(kc == 0), stop=False)
                                            ins = t.matmul(
                                                p[:, 0:512], k.onesr[:],
                                                bv_sb[:, b * 512: b * 512 + 512],
                                                start=False, stop=True)
                                        s_mm.inc(ins)
                                        idx += 1
                            for _ in range(4):
                                s_mm.inc(t.matmul(ps_pad[:, 0:512], k.identb[:],
                                                  xn[:, 0:512], start=True,
                                                  stop=True))

                        @blk.scalar
                        def _(scl):
                            total = 3 * NSEC
                            idx = 0
                            for sec in range(2):
                                bc = 0 if sec == 0 else 8
                                dst = q_fm if sec == 0 else k_fm
                                for mc in range(DC):
                                    for g in range(NG):
                                        s_mm.w(scl, idx + 1)
                                        s_ev.inc(scl.activation(
                                            dst[:, mc * T + g * 512:
                                                mc * T + g * 512 + 512],
                                            ps[idx % 3][:, 0:512], AF.Identity,
                                            bias=vecs[:, bc + mc:bc + mc + 1]))
                                        idx += 1

                        @blk.vector
                        def _(v):
                            total = 3 * NSEC
                            idx = 2 * NSEC
                            for tt in range(16):
                                for hf in range(2):
                                    s_mm.w(v, idx + 1)
                                    s_ev.inc(v.tensor_copy(
                                        v_tm[:, tt * 1024 + hf * 512:
                                             tt * 1024 + hf * 512 + 512],
                                        ps[idx % 3][:, 0:512]))
                                    idx += 1
                    k.rel(s_w, s_mm, s_ev)

            with ExitStack() as sc3, ExitStack() as psc:
                e3 = sc3.enter_context
                qsq = e3(nc.sbuf_tensor(_nm(), [128, 2 * T], BF16))
                rows_f = e3(nc.sbuf_tensor(_nm(), [32, T], F32))
                rows_s = e3(nc.sbuf_tensor(_nm(), [32, T], F32))
                rows_b = e3(nc.sbuf_tensor(_nm(), [32, T], BF16))
                pe = psc.enter_context
                ps_st = pe(nc.psum_tensor(_nm(), [32, T], F32))
                ps_bc = [pe(nc.psum_tensor(_nm(), [128, 512], F32))
                         for _ in range(3)]
                s_sq, s_stmm, s_row, s_bcmm, s_app = k.sems(5)

                with nc.Block() as blk:
                    @blk.scalar
                    def _(scl):
                        for c in range(16):
                            src = q_fm if c < DC else k_fm
                            cc = c % DC
                            if c >= 2:
                                s_stmm.w(scl, 4 * (c - 1))
                            s_sq.inc(scl.activation(
                                qsq[:, (c % 2) * T:(c % 2) * T + T],
                                src[:, cc * T:(cc + 1) * T], AF.Square))
                        s_row.w(scl, 1)
                        scl.activation(rows_s[:], rows_f[:], AF.Ln)
                        s_row.inc(scl.activation(rows_b[:], rows_s[:], AF.Exp,
                                                 scale=-0.5), 2)

                    @blk.tensor
                    def _(t):
                        for c in range(16):
                            s_sq.w(t, c + 1)
                            for g in range(NG):
                                ins = t.matmul(
                                    ps_st[:, g * 512:(g + 1) * 512],
                                    k.mask16[:, c * 32:(c + 1) * 32],
                                    qsq[:, (c % 2) * T + g * 512:
                                        (c % 2) * T + g * 512 + 512],
                                    start=(c == 0), stop=(c == 15))
                                s_stmm.inc(ins)
                        for _ in range(4):
                            s_stmm.inc(t.matmul(ps_bc[0][:, 0:512], k.identb[:],
                                                qsq[:, 0:512], start=True,
                                                stop=True))
                        s_row.w(t, 3)
                        for c in range(16):
                            for g in range(NG):
                                idx = c * NG + g
                                if idx >= 3:
                                    s_app.w(t, idx - 2)
                                s_bcmm.inc(t.matmul(
                                    ps_bc[idx % 3][:, 0:512],
                                    k.sel16[:, c * 128:(c + 1) * 128],
                                    rows_b[:, g * 512:(g + 1) * 512],
                                    start=True, stop=True))
                        for _ in range(4):
                            s_bcmm.inc(t.matmul(ps_bc[0][:, 0:512] if False
                                                else ps_st[:, 0:512],
                                                k.mask16[:, 0:32],
                                                qsq[:, 0:512], start=True,
                                                stop=True))

                    @blk.vector
                    def _(v):
                        s_stmm.w(v, 4 * 16 + 4)
                        s_row.inc(v.tensor_scalar(rows_f[:], ps_st[:, 0:T],
                                                  1.0 / 64.0, float(RMS_EPS),
                                                  ALU.mult, ALU.add))
                        for c in range(16):
                            src = q_fm if c < DC else k_fm
                            cc = c % DC
                            wcol = 16 if c < DC else 17
                            for g in range(NG):
                                idx = c * NG + g
                                s_bcmm.w(v, idx + 1)
                                sl = slice(cc * T + g * 512, cc * T + g * 512 + 512)
                                s_app.inc(v.scalar_tensor_tensor(
                                    src[:, sl], src[:, sl],
                                    vecs[:, wcol:wcol + 1],
                                    ps_bc[idx % 3][:, 0:512],
                                    ALU.mult, ALU.mult))
                k.rel(s_sq, s_stmm, s_row, s_bcmm, s_app)

            with ExitStack() as sc4, ExitStack() as psc:
                e4 = sc4.enter_context
                exp_sb = [e4(nc.sbuf_tensor(_nm(), [128, T], BF16)) for _ in range(2)]
                expT_sb = [e4(nc.sbuf_tensor(_nm(), [128, T], BF16)) for _ in range(2)]
                nmt4 = e4(nc.sbuf_tensor(_nm(), [128, 8], F32))
                nmt4b = e4(nc.sbuf_tensor(_nm(), [128, 8], F32))
                nm = [e4(nc.sbuf_tensor(_nm(), [128, 1], F32)) for _ in range(2)]
                kz = [e4(nc.sbuf_tensor(_nm(), [128, T], BF16)) for _ in range(2)]
                pe = psc.enter_context
                ps_sc = [pe(nc.psum_tensor(_nm(), [128, 512], F32))
                         for _ in range(6)]
                ps_av = [pe(nc.psum_tensor(_nm(), [128, 65], F32)) for _ in range(2)]
                NI = 256
                s_sc, s_mx4, s_nmb, s_nm, s_exp, s_trev0, s_trev1, \
                    s_avmm, s_zev, s_kz = k.sems(10)
                s_trev = {0: s_trev0, 1: s_trev1}
                exp3 = [e[:].rearrange("p (t x) -> p t x", x=128)
                        for e in exp_sb]
                expT3 = [e[:].rearrange("p (t x) -> p t x", x=128)
                         for e in expT_sb]

                with nc.Block() as blk:
                    @blk.sync
                    def _(q):
                        for J in range(NI):
                            for kh in (0, 1):
                                s_exp.w(q, 4 * J + 2 * (kh + 1))
                                if J >= 2:
                                    s_avmm.w(q, 2 * (J - 2) + 2)
                                s_trev[kh].inc(q.dma_start(
                                    expT3[J % 2][:, kh * 8:kh * 8 + 8, :],
                                    exp3[J % 2][:, kh * 8:kh * 8 + 8, :],
                                    transpose=True), 16)

                    @blk.tensor
                    def _(t):
                        for I in range(NI + 2):
                            if I < NI:
                                h_ = I >> 4
                                qt = I & 15
                                ch = h_ >> 1
                                qsl = slice(ch * T + qt * 128, ch * T + qt * 128 + 128)
                                s_kz.w(t, h_ + 1)
                                for gq in range(4):
                                    g = 4 * I + gq
                                    if g >= 6:
                                        s_exp.w(t, g - 5)
                                    s_sc.inc(t.matmul(
                                        ps_sc[g % 6][:, 0:512],
                                        q_fm[:, qsl],
                                        kz[h_ % 2][:, gq * 512:gq * 512 + 512],
                                        start=True, stop=True))
                            Jv = I - 2
                            if 0 <= Jv < NI:
                                h2 = Jv >> 4
                                for kh in (0, 1):
                                    s_trev[kh].w(t, 16 * (Jv + 1))
                                    if kh == 0 and Jv >= 2:
                                        s_zev.w(t, Jv - 1)
                                    for j in range(8):
                                        jt = kh * 8 + j
                                        t.matmul(
                                            ps_av[Jv % 2][:, 0:64],
                                            expT_sb[Jv % 2][:, jt * 128:jt * 128 + 128],
                                            v_tm[:, jt * 1024 + h2 * 64:
                                                 jt * 1024 + h2 * 64 + 64],
                                            start=(kh == 0 and j == 0), stop=False)
                                        ins = t.matmul(
                                            ps_av[Jv % 2][:, 64:65],
                                            expT_sb[Jv % 2][:, jt * 128:jt * 128 + 128],
                                            k.onesc[:],
                                            start=False, stop=(kh == 1 and j == 7))
                                    s_avmm.inc(ins)
                        for _ in range(4):
                            s_avmm.inc(t.matmul(ps_sc[0][:, 0:512], k.identb[:],
                                                q_fm[:, 0:512], start=True,
                                                stop=True))

                    @blk.vector
                    def _(v):
                        def kz_build(H):
                            if H >= 2:
                                s_sc.w(v, 64 * (H - 1))
                            po = (H & 1) * 64
                            chh = H >> 1
                            s_kz.inc(v.tensor_copy(
                                kz[H % 2][po:po + 64, :],
                                k_fm[po:po + 64, chh * T:chh * T + T]))

                        v.memset(kz[0][64:128, 0:1024], 0.0)
                        v.memset(kz[0][64:128, 1024:2048], 0.0)
                        v.memset(kz[1][0:64, 0:1024], 0.0)
                        v.memset(kz[1][0:64, 1024:2048], 0.0)
                        kz_build(0)
                        kz_build(1)
                        for I in range(NI + 2):
                            if (I < NI and I % 16 == 13
                                    and 2 <= (I >> 4) + 1 < 16):
                                kz_build((I >> 4) + 1)
                            if I < NI:
                                s_nmb.w(v, I - 1)
                                for gq in range(4):
                                    s_sc.w(v, 4 * I + gq + 1)
                                    s_mx4.inc(v.reduce_max(
                                        nmt4[:, (I % 2) * 4 + gq:
                                             (I % 2) * 4 + gq + 1],
                                        ps_sc[(4 * I + gq) % 6][:, 0:512],
                                        axis=AX.X, negate=True))
                            if I < NI:
                                s_nmb.w(v, I + 1)
                                if I >= 2:
                                    s_exp.w(v, 4 * (I - 2) + 4)
                                s_nm.inc(v.tensor_reduce(
                                    nm[I % 2][:],
                                    nmt4b[:, (I % 2) * 4:(I % 2) * 4 + 4],
                                    axis=AX.X, op=ALU.min))
                            Jv = I - 2
                            if 0 <= Jv < NI:
                                h2 = Jv >> 4
                                qt2 = Jv & 15
                                s_avmm.w(v, 2 * Jv + 2)
                                v.tensor_copy(
                                    z_tm[:, qt2 * 1024 + h2 * 64:
                                         qt2 * 1024 + h2 * 64 + 64],
                                    ps_av[Jv % 2][:, 0:64])
                                s_zev.inc(v.tensor_copy(
                                    zsum[:, Jv:Jv + 1],
                                    ps_av[Jv % 2][:, 64:65]))

                    @blk.scalar
                    def _(scl):
                        for I in range(NI + 1):
                            if I < NI:
                                s_mx4.w(scl, 4 * I + 4)
                                if I >= 2:
                                    s_nm.w(scl, I - 1)
                                s_nmb.inc(scl.copy(
                                    nmt4b[:, (I % 2) * 4:(I % 2) * 4 + 4],
                                    nmt4[:, (I % 2) * 4:(I % 2) * 4 + 4]))
                            if I < NI:
                                if I >= 2:
                                    s_trev0.w(scl, 16 * (I - 1))
                                    s_trev1.w(scl, 16 * (I - 1))
                                s_nm.w(scl, I + 1)
                                for gq in range(4):
                                    s_exp.inc(scl.activation(
                                        exp_sb[I % 2][:, gq * 512:gq * 512 + 512],
                                        ps_sc[(4 * I + gq) % 6][:, 0:512],
                                        AF.Exp, bias=nm[I % 2][:]))
                k.rel(s_sc, s_mx4, s_nmb, s_nm, s_exp, s_trev0, s_trev1,
                      s_avmm, s_zev, s_kz)

            if globals().get("_DBGC"):
                dbg = {"dc_xn": (xn, BF16, [128, DC * T]),
                       "dc_q": (q_fm, BF16, [128, DC * T]),
                       "dc_k": (k_fm, BF16, [128, DC * T]),
                       "dc_v": (v_tm, BF16, [128, 16 * 1024]),
                       "dc_z": (z_tm, BF16, [128, 16 * 1024]),
                       "dc_zs": (zsum, F32, [128, 256])}
                (sdb,) = k.sems(1, dma=True)
                outs = {n: nc.dram_tensor(n, shp, dt, kind="ExternalOutput").ap()
                        for n, (_, dt, shp) in dbg.items()}
                with nc.Block() as blk:
                    @blk.gpsimd
                    def _(q):
                        for n, (tile, _, _2) in dbg.items():
                            sdb.inc(q.dma_start(outs[n], tile[:]), 16)
                        sdb.w(q, 16 * len(dbg))
                k.rel(sdb)

        with ExitStack() as sc5:
            e5 = sc5.enter_context
            z_fm = e5(nc.sbuf_tensor(_nm(), [128, DC * T], BF16))
            rc_all = e5(nc.sbuf_tensor(_nm(), [128, 256], F32))
            s_nz, = k.sems(1)
            with nc.Block() as blk:
                @blk.vector
                def _(v):
                    v.reciprocal(rc_all[:], zsum[:])
                    for ip in range(10):
                        a0, b0 = (0, 64) if ip % 2 == 0 else (64, 0)
                        v.tensor_copy(k.pad_sb[:, a0:a0 + 64],
                                      k.pad_sb[:, b0:b0 + 64])
                    for qt in range(16):
                        for h_ in range(16):
                            col = h_ * 16 + qt
                            sl = slice(qt * 1024 + h_ * 64,
                                       qt * 1024 + h_ * 64 + 64)
                            ins = v.tensor_scalar_mul(z_tm[:, sl], z_tm[:, sl],
                                                      rc_all[:, col:col + 1])
                        s_nz.inc(ins)
            with ExitStack() as psc, nc.Block() as blk:
                pe = psc.enter_context
                ps_z = [pe(nc.psum_tensor(_nm(), [128, 1024], BF16))
                        for _ in range(2)]
                ps_pad = pe(nc.psum_tensor(_nm(), [128, 512], F32))
                s_zt, s_ze = k.sems(2)

                @blk.tensor
                def _(t):
                    idx = 0
                    for c in range(DC):
                        for b in range(2):
                            s_nz.w(t, 8 * (b + 1))
                            if idx >= 2:
                                s_ze.w(t, idx - 1)
                            for j in range(8):
                                tt = b * 8 + j
                                ins = t.transpose(
                                    ps_z[idx % 2][:, j * 128:(j + 1) * 128],
                                    z_tm[:, tt * 1024 + c * 128:
                                         tt * 1024 + c * 128 + 128],
                                    k.identb[:])
                            s_zt.inc(ins)
                            idx += 1
                    for _ in range(4):
                        s_zt.inc(t.matmul(ps_pad[:, 0:512], k.identb[:],
                                          z_tm[:, 0:512], start=True, stop=True))

                @blk.vector
                def _(v):
                    idx = 0
                    for c in range(DC):
                        for b in range(2):
                            s_zt.w(v, idx + 1)
                            s_ze.inc(v.tensor_copy(
                                z_fm[:, c * T + b * 1024: c * T + b * 1024 + 1024],
                                ps_z[idx % 2][:, 0:1024]))
                            idx += 1
                k.rel(s_nz, s_zt, s_ze)

            if globals().get("_DBGC"):
                (sdb2,) = k.sems(1, dma=True)
                zfm_out = nc.dram_tensor("dc_zfm", [128, DC * T], BF16,
                                         kind="ExternalOutput").ap()
                with nc.Block() as blk:
                    @blk.gpsimd
                    def _(q):
                        sdb2.inc(q.dma_start(zfm_out, z_fm[:]), 16)
                        sdb2.w(q, 16)
                k.rel(sdb2)

            emit_resid_proj(
                k, lambda kc, g: z_fm[:, kc * T + g * 512: kc * T + g * 512 + 512],
                W["c_wo"], DC, DC, x_hbm, y_hbm, vecs, bias_col=20)
        k.rel(s_v)


def emit_l1ffn(k, x_hbm, out_hbm, W):
    nc = k.nc
    with ExitStack() as sc:
        e = sc.enter_context
        vecs = e(nc.sbuf_tensor(_nm(), [128, 136], F32))
        xn = e(nc.sbuf_tensor(_nm(), [128, DC * T], BF16))
        (s_v,) = k.sems(1, dma=True)
        with nc.Block() as blk:
            @blk.gpsimd
            def _(q):
                s_v.inc(q.dma_start(vecs[:, 0:32], W["d_s1"]), 16)
                s_v.inc(q.dma_start(vecs[:, 32:64], W["d_s2"]), 16)
                s_v.inc(q.dma_start(vecs[:, 64:96], W["d_h1s"]), 16)
                s_v.inc(q.dma_start(vecs[:, 96:128], W["d_h2s"]), 16)
                s_v.inc(q.dma_start(vecs[:, 128:136], W["d_w3s"]), 16)

        emit_ln(k, x_hbm, xn)
        with nc.Block() as blk:
            @blk.vector
            def _(v):
                s_v.w(v, 80)

        HT = 1024
        for hf in range(2):
            with ExitStack() as sch:
                eh = sch.enter_context
                a1 = eh(nc.sbuf_tensor(_nm(), [128, H1C * HT], BF16))

                def rhs_xn(kc, g, hf=hf):
                    o = kc * T + hf * HT + g * 512
                    return xn[:, o:o + 512]

                with ExitStack() as sc2, ExitStack() as psc:
                    emit_glu(k, sc2, psc, W["d_w1"], W["d_w2"], H1C, DC, a1, HT,
                             rhs_xn, ng=2, s1_col=0, s2_col=32, vecs=vecs)

                with ExitStack() as sca:
                    a2 = sca.enter_context(
                        nc.sbuf_tensor(_nm(), [128, H1C * HT], BF16))
                    for (wname, scol, src, dst) in (("d_h1", 64, a1, a2),
                                                    ("d_h2", 96, a2, a1)):
                        with ExitStack() as sc2, ExitStack() as psc:
                            pr = Proj(k, sc2, psc, W[wname], H1C, H1C,
                                      (lambda s: lambda kc, g:
                                       s[:, kc * HT + g * 512: kc * HT + g * 512 + 512])(src),
                                      ng=2)
                            ps_pad = psc.enter_context(
                                nc.psum_tensor(_nm(), [128, 512], F32))
                            with nc.Block() as blk:
                                @blk.gpsimd
                                def _(q):
                                    pr.emit_dma(q)

                                @blk.tensor
                                def _(t):
                                    pr.emit_pe(t, ps_pad)

                                @blk.scalar
                                def _(scl):
                                    def ev(eng, mc, g, ps, dst=dst, scol=scol):
                                        return eng.activation(
                                            dst[:, mc * HT + g * 512:
                                                mc * HT + g * 512 + 512],
                                            ps, AF.Gelu,
                                            scale=vecs[:, scol + mc:scol + mc + 1])
                                    pr.emit_evac(scl, ev)
                            pr.done()

                emit_final(k, a1, x_hbm, out_hbm, W["d_w3"], vecs, hf)


def emit_final(k, a3, x_hbm, out_hbm, w3_hbm, vecs, hf):
    nc = k.nc
    HT = 1024
    with ExitStack() as sc, ExitStack() as psc:
        e = sc.enter_context
        xres = e(nc.sbuf_tensor(_nm(), [128, 2 * HT], F32))
        outfm = e(nc.sbuf_tensor(_nm(), [128, 3 * 512], F32))
        stage = e(nc.sbuf_tensor(_nm(), [128, 8 * 1024], F32))
        wbuf = [e(nc.sbuf_tensor(_nm(), [128, H1C * 128], BF16)) for _ in range(2)]
        pe = psc.enter_context
        ps = [pe(nc.psum_tensor(_nm(), [128, 512], F32)) for _ in range(3)]
        ps_tr = [pe(nc.psum_tensor(_nm(), [128, 512], F32)) for _ in range(2)]
        ps_pad = pe(nc.psum_tensor(_nm(), [128, 512], F32))
        s_w, s_res, s_st = k.sems(3, dma=True)
        s_mm, s_oev, s_trmm, s_trev = k.sems(4)
        NTOT = DC * 2
        stage3 = stage[:].rearrange("p (tb x) -> p tb x", x=1024)

        with nc.Block() as blk:
            @blk.gpsimd
            def _(q):
                for mc in range(DC):
                    if mc >= 2:
                        s_oev.w(q, (mc - 1) * 2)
                    s_w.inc(q.dma_start(wbuf[mc % 2][:], w3_hbm[mc]), 16)

            @blk.sync
            def _(q):
                for mc in range(DC):
                    if mc >= 2:
                        s_oev.w(q, (mc - 1) * 2)
                    s_res.inc(q.dma_start(
                        xres[:, (mc % 2) * HT:(mc % 2) * HT + HT],
                        x_hbm[:, mc * T + hf * HT: mc * T + hf * HT + HT]), 16)
                for tb in range(8):
                    s_trev.w(q, 15 + tb // 4)
                    s_st.inc(q.dma_start(
                        out_hbm[hf * HT + tb * 128: hf * HT + tb * 128 + 128, :],
                        stage3[:, tb, :]), 16)
                s_st.w(q, 16 * 8)

            @blk.tensor
            def _(t):
                def tr_group(j):
                    s_oev.w(t, j + 1)
                    if j >= 2:
                        s_trev.w(t, j - 1)
                    for jj in range(4):
                        ins = t.transpose(
                            ps_tr[j % 2][:, jj * 128:(jj + 1) * 128],
                            outfm[:, (j % 3) * 512 + jj * 128:
                                  (j % 3) * 512 + jj * 128 + 128],
                            k.identf[:])
                    s_trmm.inc(ins)

                for mc in range(DC):
                    s_w.w(t, 16 * (mc + 1))
                    for g in range(2):
                        idx = mc * 2 + g
                        if idx >= 3:
                            s_oev.w(t, idx - 2)
                        for kc in range(H1C):
                            ins = t.matmul(
                                ps[idx % 3][:, 0:512],
                                wbuf[mc % 2][:, kc * 128:(kc + 1) * 128],
                                a3[:, kc * HT + g * 512: kc * HT + g * 512 + 512],
                                start=(kc == 0), stop=(kc == H1C - 1))
                        s_mm.inc(ins)
                        if idx - 2 >= 0:
                            tr_group(idx - 2)
                for _ in range(4):
                    s_mm.inc(t.matmul(ps_pad[:, 0:512], k.identb[:],
                                      a3[:, 0:512], start=True, stop=True))
                for j in (NTOT - 2, NTOT - 1):
                    tr_group(j)
                for _ in range(2):
                    s_trmm.inc(t.transpose(ps_pad[:, 0:128], k.identf[:],
                                           k.identf[:]))

            @blk.vector
            def _(v):
                def trev(j):
                    mc2, g2 = j // 2, j % 2
                    s_trmm.w(v, j + 1)
                    dst = stage3[:, g2 * 4:g2 * 4 + 4, mc2 * 128:mc2 * 128 + 128]
                    src = ps_tr[j % 2][:, 0:512].rearrange(
                        "p (a b) -> p a b", b=128)
                    s_trev.inc(v.tensor_copy(dst, src))

                for mc in range(DC):
                    for g in range(2):
                        idx = mc * 2 + g
                        s_mm.w(v, idx + 1)
                        s_res.w(v, 16 * (mc + 1))
                        if idx >= 3:
                            s_trmm.w(v, idx - 2)
                        s_oev.inc(v.scalar_tensor_tensor(
                            outfm[:, (idx % 3) * 512:(idx % 3) * 512 + 512],
                            ps[idx % 3][:, 0:512],
                            vecs[:, 128 + mc:128 + mc + 1],
                            xres[:, (mc % 2) * HT + g * 512:
                                 (mc % 2) * HT + g * 512 + 512],
                            ALU.mult, ALU.add))
                        if idx - 2 >= 0:
                            trev(idx - 2)
                for j in (NTOT - 2, NTOT - 1):
                    trev(j)
        k.rel(s_w, s_mm, s_res, s_oev, s_trmm, s_trev, s_st)


def _host_prep(inputs):
    f32 = np.float32
    g = lambda n: np.asarray(inputs[n])
    Wqkv = g("L1_attn_Wqkv").astype(f32)
    bqkv = g("L1_attn_bqkv").astype(f32)
    prep = {
        "a_wg": _prep_w(g("L0_attn_Wg")), "a_wbx": _prep_w(g("L0_attn_WB")),
        "a_wc": _prep_w(g("L0_attn_WC")), "a_wout": _prep_w(g("L0_attn_Wout")),
        "a_D": _prep_pc(g("L0_attn_D")), "a_bout": _prep_pc(g("L0_attn_bout")),
        "a_gamma": _prep_pc(g("L0_attn_gamma")),
        "a_beta": _prep_pc(g("L0_attn_beta")),
        "a_Alog": _prep_pc(g("L0_attn_A_log")), "a_bg": _prep_pc(g("L0_attn_bg")),
        "f_wgate": _prep_w(g("L0_ffn_w_gate")), "f_wup": _prep_w(g("L0_ffn_w_up")),
        "f_wdown": _prep_w(g("L0_ffn_w_down")),
        "c_wq": _prep_w(Wqkv[0:1024]), "c_wk": _prep_w(Wqkv[1024:2048]),
        "c_wv": _prep_w(Wqkv[2048:3072]),
        "c_bq": _prep_pc(bqkv[0:1024]), "c_bk": _prep_pc(bqkv[1024:2048]),
        "c_bv": np.ascontiguousarray(bqkv[2048:3072].reshape(1, 1024)),
        "c_wqg": np.ascontiguousarray(
            np.concatenate([g("L1_attn_wq")] * 2).reshape(128, 1).astype(f32)),
        "c_wkg": np.ascontiguousarray(
            np.concatenate([g("L1_attn_wk")] * 2).reshape(128, 1).astype(f32)),
        "c_wo": _prep_w(g("L1_attn_Wo")), "c_bo": _prep_pc(g("L1_attn_bo")),
        "d_w1": _prep_w(g("L1_ffn_w1_w")), "d_w2": _prep_w(g("L1_ffn_w2_w")),
        "d_h1": _prep_w(g("L1_ffn_h1_w")), "d_h2": _prep_w(g("L1_ffn_h2_w")),
        "d_w3": _prep_w(g("L1_ffn_w3_w")),
        "d_s1": _prep_pc(g("L1_ffn_w1_s")), "d_s2": _prep_pc(g("L1_ffn_w2_s")),
        "d_h1s": _prep_pc(g("L1_ffn_h1_s")), "d_h2s": _prep_pc(g("L1_ffn_h2_s")),
        "d_w3s": _prep_pc(g("L1_ffn_w3_s")),
    }
    return prep


_BLOCK_WNAMES = {
    "A": ["a_wg", "a_wbx", "a_wc", "a_wout", "a_D", "a_bout", "a_gamma",
          "a_beta", "a_Alog", "a_bg"],
    "B": ["f_wgate", "f_wup", "f_wdown"],
    "C": ["c_wq", "c_wk", "c_wv", "c_bq", "c_bk", "c_bv", "c_wqg", "c_wkg",
          "c_wo", "c_bo"],
    "D": ["d_w1", "d_w2", "d_h1", "d_h2", "d_w3", "d_s1", "d_s2", "d_h1s",
          "d_h2s", "d_w3s"],
}

_EMITTERS = {"A": emit_l0attn, "B": emit_l0ffn, "C": emit_l1attn,
             "D": emit_l1ffn}


def build_block(which, prep):
    nc = bass.Bass("TRN2", target_bir_lowering=False,
                  detect_race_conditions=False)
    dt_of = lambda a: {np.dtype(np.float32): F32, np.dtype(np.int32): I32}[a.dtype]
    W = {n: nc.dram_tensor(n, list(prep[n].shape), dt_of(prep[n]),
                           kind="ExternalInput").ap()
         for n in _BLOCK_WNAMES[which]}
    x_in = nc.dram_tensor("x_in", [128, DC * T], F32, kind="ExternalInput").ap()
    if which == "D":
        x_out = nc.dram_tensor("x_out", [T, D], F32, kind="ExternalOutput").ap()
    else:
        x_out = nc.dram_tensor("x_out", [128, DC * T], F32,
                               kind="ExternalOutput").ap()
    with ExitStack() as stack:
        k = K(nc, stack)
        k.load_consts()
        _EMITTERS[which](k, x_in, x_out, W)
    return nc


def build_fused(prep):
    nc = bass.Bass("TRN2", target_bir_lowering=False,
                  detect_race_conditions=False)
    dt_of = lambda a: {np.dtype(np.float32): F32, np.dtype(np.int32): I32}[a.dtype]
    W = {n: nc.dram_tensor(n, list(prep[n].shape), dt_of(prep[n]),
                           kind="ExternalInput").ap()
         for n in prep}
    x_in = nc.dram_tensor("x_in", [128, DC * T], F32, kind="ExternalInput").ap()
    xA = nc.dram_tensor("xA", [128, DC * T], F32).ap()
    xB = nc.dram_tensor("xB", [128, DC * T], F32).ap()
    xC = nc.dram_tensor("xC", [128, DC * T], F32).ap()
    out = nc.dram_tensor("x_out", [T, D], F32, kind="ExternalOutput").ap()
    with ExitStack() as stack:
        k = K(nc, stack)
        k.load_consts()
        emit_l0attn(k, x_in, xA, W)
        emit_l0ffn(k, xA, xB, W)
        emit_l1attn(k, xB, xC, W)
        emit_l1ffn(k, xC, out, W)
    return nc


FUSED = True


def kernel(**inputs):
    prep = _host_prep(inputs)
    x0 = np.asarray(inputs["L0_attn_x"]).astype(np.float32)
    xs = [_fm(x0[b]) for b in range(8)]
    if FUSED:
        nc = build_fused(prep)
        in_maps = [{**prep, "x_in": xs[b]} for b in range(8)]
        res = run_bass_kernel_spmd(nc, in_maps, core_ids=list(range(8)))
        return np.stack([res.results[b]["x_out"] for b in range(8)], axis=0)
    for which in ("A", "B", "C", "D"):
        nc = build_block(which, prep)
        wmap = {n: prep[n] for n in _BLOCK_WNAMES[which]}
        in_maps = [{**wmap, "x_in": xs[b]} for b in range(8)]
        res = run_bass_kernel_spmd(nc, in_maps, core_ids=list(range(8)))
        xs = [res.results[b]["x_out"] for b in range(8)]
    return np.stack(xs, axis=0)


# version 21
# speedup vs baseline: 1.0461x; 1.0461x over previous
import numpy as np
from contextlib import ExitStack

import concourse.bass as bass
import concourse.mybir as mybir
from concourse.bass_utils import run_bass_kernel_spmd

F32 = mybir.dt.float32
BF16 = mybir.dt.bfloat16
I32 = mybir.dt.int32
AF = mybir.ActivationFunctionType
ALU = mybir.AluOpType
AX = mybir.AxisListType

T = 2048
D = 1024
DC = D // 128
NG = T // 512
S = 64
H0C = 3072 // 128
H1C = 4096 // 128
NH, DH = 16, 64
LN_EPS = 1e-5
RMS_EPS = 1e-6
QS = 1.0 / 127.0

_NMC = [0]


def _nm():
    _NMC[0] += 1
    return f"t{_NMC[0]}"


def _prep_w(W):
    W = np.ascontiguousarray(W)
    Dout, Din = W.shape
    mb = 128 if Dout % 128 == 0 else Dout
    kb = 128 if Din % 128 == 0 else Din
    Mc, Kc = Dout // mb, Din // kb
    H = W.reshape(Mc, mb, Kc, kb).transpose(0, 3, 2, 1)
    return np.ascontiguousarray(H.reshape(Mc, kb, Kc * mb))


def _prep_pc(v):
    n = v.shape[0]
    if n >= 128:
        return np.ascontiguousarray(v.reshape(n // 128, 128).T)
    return np.ascontiguousarray(v.reshape(n, 1))


def _fm(x):
    t, d = x.shape
    return np.ascontiguousarray(
        x.reshape(t, d // 128, 128).transpose(2, 1, 0).reshape(128, (d // 128) * t))


class Sem:
    def __init__(self, h):
        self.h = h
        self.base = 0
        self.cnt = 0

    def w(self, eng, rel):
        if rel > 0:
            eng.wait_ge(self.h, self.base + rel)

    def inc(self, ins, n=1):
        ins.then_inc(self.h, n)
        self.cnt += n
        return ins


class K:
    def __init__(self, nc, stack):
        self.nc = nc
        self.st = stack
        self._free = []
        self._nsem = 0
        import ml_dtypes
        bf = ml_dtypes.bfloat16
        self.i_identb = nc.inline_tensor(np.eye(128, dtype=bf), "c_identb")
        self.i_identf = nc.inline_tensor(np.eye(128, dtype=np.float32), "c_identf")
        self.i_onesc = nc.inline_tensor(np.ones((128, 1), dtype=bf), "c_onesc")
        self.i_onesr = nc.inline_tensor(np.ones((1, 128), dtype=bf), "c_onesr")
        mask = np.zeros((128, 16 * 32), dtype=bf)
        sel = np.zeros((32, 16 * 128), dtype=bf)
        for c in range(16):
            mask[0:64, c * 32 + 2 * c] = 1.0
            mask[64:128, c * 32 + 2 * c + 1] = 1.0
            sel[2 * c, c * 128:c * 128 + 64] = 1.0
            sel[2 * c + 1, c * 128 + 64:c * 128 + 128] = 1.0
        self.i_mask16 = nc.inline_tensor(mask, "c_mask16")
        self.i_sel16 = nc.inline_tensor(sel, "c_sel16")
        e = stack.enter_context
        self.identb = e(nc.sbuf_tensor("cs_identb", [128, 128], BF16))
        self.identf = e(nc.sbuf_tensor("cs_identf", [128, 128], F32))
        self.onesc = e(nc.sbuf_tensor("cs_onesc", [128, 1], BF16))
        self.onesr = e(nc.sbuf_tensor("cs_onesr", [1, 128], BF16))
        self.mask16 = e(nc.sbuf_tensor("cs_mask16", [128, 16 * 32], BF16))
        self.sel16 = e(nc.sbuf_tensor("cs_sel16", [32, 16 * 128], BF16))
        self.pad_sb = e(nc.sbuf_tensor("cs_pad", [128, 1024], F32))

    def delay(self, eng, n=4):
        for i in range(n):
            a, b = (0, 512) if i % 2 == 0 else (512, 0)
            eng.tensor_copy(self.pad_sb[:, a:a + 512],
                            self.pad_sb[:, b:b + 512])

    def delay_act(self, eng, n=4):
        for i in range(n):
            a, b = (0, 512) if i % 2 == 0 else (512, 0)
            eng.copy(self.pad_sb[:, a:a + 512], self.pad_sb[:, b:b + 512])

    def sems(self, n, dma=False):
        out = []
        for _ in range(n):
            if self._free and not dma:
                out.append(self._free.pop())
            else:
                self._nsem += 1
                h = self.st.enter_context(self.nc.semaphore(f"s{self._nsem}"))
                s = Sem(h)
                s.dma = dma
                out.append(s)
        return out

    def rel(self, *sems):
        for s in sems:
            s.base += s.cnt
            s.cnt = 0
            if not getattr(s, "dma", False):
                self._free.append(s)

    def load_consts(self):
        nc = self.nc
        (s,) = self.sems(1, dma=True)
        with nc.Block() as blk:
            @blk.sync
            def _(q):
                for dst, src in ((self.identb, self.i_identb),
                                 (self.identf, self.i_identf),
                                 (self.onesc, self.i_onesc),
                                 (self.onesr, self.i_onesr),
                                 (self.mask16, self.i_mask16),
                                 (self.sel16, self.i_sel16)):
                    s.inc(q.dma_start(dst[:], src.ap()), 16)
                s.w(q, 96)

            @blk.vector
            def _(v):
                v.memset(self.pad_sb[:], 0.0)
        self.rel(s)


def _wload(q, dst_sb, w_ap):
    Mc, kb, X = w_ap.shape
    dst3 = dst_sb[:].rearrange("p (m x) -> p m x", x=X)
    return q.dma_start(dst3, w_ap.rearrange("m p x -> p m x"))


def emit_ln(k, x_hbm, xn, gamma_col=None, beta_col=None, vecs=None, h_src=None):
    nc = k.nc
    with ExitStack() as sc:
        e = sc.enter_context
        xb = e(nc.sbuf_tensor(_nm(), [128, DC * T], BF16)) if h_src is None else h_src
        xsq = e(nc.sbuf_tensor(_nm(), [128, DC * 512], BF16))
        rows = e(nc.sbuf_tensor(_nm(), [1, 3 * T], F32))
        mu = rows[:, 0:T]
        e2 = rows[:, T:2 * T]
        t3 = rows[:, 2 * T:3 * T]
        rrow = e(nc.sbuf_tensor(_nm(), [1, T], BF16))
        mrow = e(nc.sbuf_tensor(_nm(), [1, T], BF16))
        rb = e(nc.sbuf_tensor(_nm(), [128, T], BF16))
        mb_ = e(nc.sbuf_tensor(_nm(), [128, T], BF16))
        s_ld, s_ld2 = k.sems(2, dma=True)
        s_sq, s_stat, s_sev, s_rows, s_bc, s_bev = k.sems(6)

        with ExitStack() as psc:
            pe = psc.enter_context
            ps_sum = pe(nc.psum_tensor(_nm(), [1, 512], F32))
            ps_sq = pe(nc.psum_tensor(_nm(), [1, 512], F32))
            ps_pad = pe(nc.psum_tensor(_nm(), [128, 512], F32))
            ps_r = [pe(nc.psum_tensor(_nm(), [128, 512], F32)) for _ in range(2)]
            ps_m = [pe(nc.psum_tensor(_nm(), [128, 512], F32)) for _ in range(2)]

            with nc.Block() as blk:
                if h_src is None:
                    xb3 = xb[:].rearrange("p (c t) -> p c t", t=T)
                    x3 = x_hbm.rearrange("p (c t) -> p c t", t=T)

                    @blk.gpsimd
                    def _(q):
                        s_ld.inc(q.dma_start(xb3[:, :, 0:1024],
                                             x3[:, :, 0:1024]), 16)
                        s_ld2.inc(q.dma_start(xb3[:, :, 1024:2048],
                                              x3[:, :, 1024:2048]), 16)

                @blk.scalar
                def _(scl):
                    for g in range(NG):
                        if h_src is None:
                            s_ld.w(scl, 16)
                            if g >= 2:
                                s_ld2.w(scl, 16)
                        if g > 0:
                            s_stat.w(scl, 2 * g)
                        for c in range(DC):
                            s_sq.inc(scl.activation(
                                xsq[:, c * 512:(c + 1) * 512],
                                xb[:, c * T + g * 512: c * T + g * 512 + 512],
                                AF.Square))

                @blk.tensor
                def _(t):
                    for g in range(NG):
                        if h_src is None:
                            s_ld.w(t, 16)
                            if g >= 2:
                                s_ld2.w(t, 16)
                        if g > 0:
                            s_sev.w(t, 2 * g)
                        for c in range(DC):
                            ins = t.matmul(
                                ps_sum[:, 0:512], k.onesc[:],
                                xb[:, c * T + g * 512: c * T + g * 512 + 512],
                                start=(c == 0), stop=(c == DC - 1))
                        s_stat.inc(ins)
                        for c in range(DC):
                            s_sq.w(t, g * DC + c + 1)
                            ins = t.matmul(
                                ps_sq[:, 0:512], k.onesc[:],
                                xsq[:, c * 512:(c + 1) * 512],
                                start=(c == 0), stop=(c == DC - 1))
                        s_stat.inc(ins)
                    for _ in range(4):
                        s_stat.inc(t.matmul(ps_pad[:, 0:512], k.identb[:],
                                            xb[:, 0:512], start=True, stop=True))

                @blk.vector
                def _(v):
                    for g in range(NG):
                        s_stat.w(v, 2 * g + 2)
                        v.tensor_copy(mu[:, g * 512:(g + 1) * 512],
                                      ps_sum[:, 0:512])
                        s_sev.inc(v.tensor_copy(e2[:, g * 512:(g + 1) * 512],
                                                ps_sq[:, 0:512]), 2)
                    v.tensor_scalar_mul(mu, mu, 1.0 / D)
                    v.tensor_scalar(e2, e2, 1.0 / D, float(LN_EPS),
                                    ALU.mult, ALU.add)
                    v.tensor_mul(t3, mu, mu)
                    s_rows.inc(v.tensor_sub(e2, e2, t3))

            with nc.Block() as blk:
                @blk.scalar
                def _(scl):
                    s_rows.w(scl, 1)
                    scl.activation(t3, e2, AF.Ln)
                    s_rows.inc(scl.activation(rrow[:], t3, AF.Exp, scale=-0.5),
                               1)

                @blk.vector
                def _(v):
                    s_rows.w(v, 2)
                    v.tensor_copy(e2, rrow[:])
                    v.tensor_mul(t3, mu, e2)
                    s_rows.inc(v.tensor_scalar_mul(mrow[:], t3, -1.0))

            with nc.Block() as blk:
                @blk.tensor
                def _(t):
                    s_rows.w(t, 3)
                    for g in range(NG):
                        if g >= 2:
                            s_bev.w(t, 2 * (g - 1))
                        s_bc.inc(t.matmul(ps_r[g % 2][:, 0:512], k.onesr[:],
                                          rrow[:, g * 512:(g + 1) * 512],
                                          start=True, stop=True))
                        s_bc.inc(t.matmul(ps_m[g % 2][:, 0:512], k.onesr[:],
                                          mrow[:, g * 512:(g + 1) * 512],
                                          start=True, stop=True))

                @blk.scalar
                def _(scl):
                    for g in range(NG):
                        s_bc.w(scl, 2 * g + 2)
                        scl.copy(rb[:, g * 512:(g + 1) * 512], ps_r[g % 2][:, 0:512])
                        s_bev.inc(scl.copy(mb_[:, g * 512:(g + 1) * 512],
                                           ps_m[g % 2][:, 0:512]), 2)

                @blk.vector
                def _(v):
                    for g in range(NG):
                        s_bev.w(v, 2 * (g + 1))
                        for c in range(DC):
                            sl = slice(c * T + g * 512, c * T + g * 512 + 512)
                            gl = slice(g * 512, g * 512 + 512)
                            v.tensor_mul(xn[:, sl], xb[:, sl], rb[:, gl])
                            v.tensor_add(xn[:, sl], xn[:, sl], mb_[:, gl])
                            if gamma_col is not None:
                                v.tensor_scalar(
                                    xn[:, sl], xn[:, sl],
                                    vecs[:, gamma_col + c:gamma_col + c + 1],
                                    vecs[:, beta_col + c:beta_col + c + 1],
                                    ALU.mult, ALU.add)
        k.rel(s_ld, s_ld2, s_sq, s_stat, s_sev, s_rows, s_bc, s_bev)


class Proj:
    def __init__(self, k, scope, psc, w_hbm, Mc, Kc, rhs_fn, ng=NG):
        self.k, self.w = k, w_hbm
        self.Mc, self.Kc, self.rhs_fn, self.ng = Mc, Kc, rhs_fn, ng
        nc = k.nc
        self.wbuf = [scope.enter_context(nc.sbuf_tensor(_nm(), [128, Kc * 128], BF16))
                     for _ in range(2)]
        self.ps = [psc.enter_context(nc.psum_tensor(_nm(), [128, 512], F32))
                   for _ in range(3)]
        (self.s_w,) = k.sems(1, dma=True)
        self.s_mm, self.s_ev = k.sems(2)

    def emit_dma(self, q):
        for mc in range(self.Mc):
            if mc >= 2:
                self.s_ev.w(q, (mc - 1) * self.ng)
            self.s_w.inc(q.dma_start(self.wbuf[mc % 2][:], self.w[mc]), 16)

    def emit_pe(self, t, pad_ps):
        for mc in range(self.Mc):
            self.s_w.w(t, 16 * (mc + 1))
            for g in range(self.ng):
                idx = mc * self.ng + g
                if idx >= 3:
                    self.s_ev.w(t, idx - 2)
                for kc in range(self.Kc):
                    ins = t.matmul(
                        self.ps[idx % 3][:, 0:512],
                        self.wbuf[mc % 2][:, kc * 128:(kc + 1) * 128],
                        self.rhs_fn(kc, g), start=(kc == 0),
                        stop=(kc == self.Kc - 1))
                self.s_mm.inc(ins)
        for _ in range(4):
            self.s_mm.inc(t.matmul(pad_ps[:, 0:512], self.k.identb[:],
                                   self.wbuf[0][:, 0:512], start=True, stop=True))

    def emit_evac(self, eng, evac_fn):
        for mc in range(self.Mc):
            for g in range(self.ng):
                idx = mc * self.ng + g
                self.s_mm.w(eng, idx + 1)
                self.s_ev.inc(evac_fn(eng, mc, g, self.ps[idx % 3][:, 0:512]))

    def done(self):
        self.k.rel(self.s_w, self.s_mm, self.s_ev)


def emit_resid_proj(k, src_fn, w_hbm, Mc, Kc, res_hbm, out_hbm, vecs,
                    bias_col=None, scale_col=None, scale_imm=None):
    nc = k.nc
    with ExitStack() as sc, ExitStack() as psc:
        e = sc.enter_context
        xres = e(nc.sbuf_tensor(_nm(), [128, 2 * T], F32))
        x1 = e(nc.sbuf_tensor(_nm(), [128, 2 * T], F32))
        s_res, s_st = k.sems(2, dma=True)
        pr = Proj(k, sc, psc, w_hbm, Mc, Kc, src_fn)
        ps_pad = psc.enter_context(nc.psum_tensor(_nm(), [128, 512], F32))

        with nc.Block() as blk:
            @blk.gpsimd
            def _(q):
                pr.emit_dma(q)

            @blk.sync
            def _(q):
                for mc in range(Mc):
                    if mc >= 2:
                        pr.s_ev.w(q, (mc - 1) * NG)
                        s_st.inc(q.dma_start(
                            out_hbm[:, (mc - 2) * T:(mc - 1) * T],
                            x1[:, (mc % 2) * T:(mc % 2) * T + T]), 16)
                    s_res.inc(q.dma_start(
                        xres[:, (mc % 2) * T:(mc % 2) * T + T],
                        res_hbm[:, mc * T:(mc + 1) * T]), 16)
                for mc in (Mc - 2, Mc - 1):
                    pr.s_ev.w(q, (mc + 1) * NG)
                    s_st.inc(q.dma_start(
                        out_hbm[:, mc * T:(mc + 1) * T],
                        x1[:, (mc % 2) * T:(mc % 2) * T + T]), 16)
                s_st.w(q, 16 * Mc)

            @blk.tensor
            def _(t):
                pr.emit_pe(t, ps_pad)

            @blk.vector
            def _(v):
                def evac(eng, mc, g, ps):
                    s_res.w(eng, 16 * (mc + 1))
                    if mc >= 2 and g == 0:
                        s_st.w(eng, 16 * (mc - 1))
                    sl = slice((mc % 2) * T + g * 512,
                               (mc % 2) * T + g * 512 + 512)
                    if bias_col is not None:
                        return eng.scalar_tensor_tensor(
                            x1[:, sl], ps,
                            vecs[:, bias_col + mc:bias_col + mc + 1],
                            xres[:, sl], ALU.add, ALU.add)
                    if scale_col is not None:
                        return eng.scalar_tensor_tensor(
                            x1[:, sl], ps,
                            vecs[:, scale_col + mc:scale_col + mc + 1],
                            xres[:, sl], ALU.mult, ALU.add)
                    return eng.scalar_tensor_tensor(
                        x1[:, sl], ps, float(scale_imm), xres[:, sl],
                        ALU.mult, ALU.add)
                pr.emit_evac(v, evac)
        pr.done()
        k.rel(s_res, s_st)


def emit_l0attn(k, x_hbm, y_hbm, W):
    nc = k.nc
    with ExitStack() as sc:
        e = sc.enter_context
        vecs = e(nc.sbuf_tensor(_nm(), [128, 48], F32))
        xn = e(nc.sbuf_tensor(_nm(), [128, DC * T], BF16))
        hn = e(nc.sbuf_tensor(_nm(), [128, DC * T], BF16))
        (s_v,) = k.sems(1, dma=True)
        with nc.Block() as blk:
            @blk.gpsimd
            def _(q):
                s_v.inc(q.dma_start(vecs[:, 0:8], W["a_D"]), 16)
                s_v.inc(q.dma_start(vecs[:, 8:16], W["a_bout"]), 16)
                s_v.inc(q.dma_start(vecs[:, 16:24], W["a_gamma"]), 16)
                s_v.inc(q.dma_start(vecs[:, 24:32], W["a_beta"]), 16)
                s_v.inc(q.dma_start(vecs[0:64, 32:33], W["a_Alog"]), 16)
                s_v.inc(q.dma_start(vecs[0:64, 33:34], W["a_bg"]), 16)

        emit_ln(k, x_hbm, xn)

        with ExitStack() as sc2:
            e2 = sc2.enter_context
            wg_sb = e2(nc.sbuf_tensor(_nm(), [128, DC * S], BF16))
            wbx_sb = e2(nc.sbuf_tensor(_nm(), [128, DC * S], BF16))
            wc_sb = e2(nc.sbuf_tensor(_nm(), [64, DC * 128], BF16))
            gate_sb = e2(nc.sbuf_tensor(_nm(), [64, T], BF16))
            u_sb = e2(nc.sbuf_tensor(_nm(), [64, T], F32))
            ab_sb = e2(nc.sbuf_tensor(_nm(), [64, T], F32))
            states_b = e2(nc.sbuf_tensor(_nm(), [64, T], BF16))
            h = e2(nc.sbuf_tensor(_nm(), [128, DC * T], BF16))
            (s_w2,) = k.sems(1, dma=True)
            s_gb, s_gev, s_u, s_scan, s_ymm, s_yev, s_aexp = k.sems(7)

            with ExitStack() as psc, nc.Block() as blk:
                pe = psc.enter_context
                ps_g = [pe(nc.psum_tensor(_nm(), [64, 512], F32)) for _ in range(2)]
                ps_b = [pe(nc.psum_tensor(_nm(), [64, 512], F32)) for _ in range(2)]
                ps_pad = pe(nc.psum_tensor(_nm(), [128, 512], F32))

                @blk.gpsimd
                def _(q):
                    s_w2.inc(_wload(q, wg_sb, W["a_wg"]), 16)
                    s_w2.inc(_wload(q, wbx_sb, W["a_wbx"]), 16)
                    s_w2.inc(_wload(q, wc_sb, W["a_wc"]), 16)

                @blk.scalar
                def _(scl):
                    s_v.w(scl, 96)
                    s_aexp.inc(scl.activation(vecs[0:64, 32:33],
                                              vecs[0:64, 32:33], AF.Exp))
                    for g in range(NG):
                        s_gb.w(scl, 2 * g + 1)
                        s_gev.inc(scl.activation(
                            gate_sb[:, g * 512:(g + 1) * 512],
                            ps_g[g % 2][0:64, 0:512], AF.Sigmoid,
                            bias=vecs[0:64, 33:34]))

                @blk.tensor
                def _(t):
                    s_w2.w(t, 32)
                    for g in range(NG):
                        if g >= 2:
                            s_u.w(t, g - 1)
                        for kc in range(DC):
                            ins = t.matmul(
                                ps_g[g % 2][0:64, 0:512],
                                wg_sb[:, kc * S:(kc + 1) * S],
                                xn[:, kc * T + g * 512: kc * T + g * 512 + 512],
                                start=(kc == 0), stop=(kc == DC - 1))
                        s_gb.inc(ins)
                        for kc in range(DC):
                            ins = t.matmul(
                                ps_b[g % 2][0:64, 0:512],
                                wbx_sb[:, kc * S:(kc + 1) * S],
                                xn[:, kc * T + g * 512: kc * T + g * 512 + 512],
                                start=(kc == 0), stop=(kc == DC - 1))
                        s_gb.inc(ins)
                    for _ in range(4):
                        s_gb.inc(t.matmul(ps_pad[:, 0:512], k.identb[:],
                                          xn[:, 0:512], start=True, stop=True))

                @blk.vector
                def _(v):
                    s_v.w(v, 96)
                    s_aexp.w(v, 1)
                    v.tensor_scalar(ab_sb[0:64, :], xn[0:64, 0:T], 0.0,
                                    vecs[0:64, 32:33], ALU.mult, ALU.add)
                    v.tensor_scalar_add(vecs[:, 34:42], vecs[:, 0:8], 1.0)
                    for g in range(NG):
                        s_gev.w(v, g + 1)
                        s_gb.w(v, 2 * g + 2)
                        s_u.inc(v.tensor_mul(
                            u_sb[0:64, g * 512:(g + 1) * 512],
                            gate_sb[0:64, g * 512:(g + 1) * 512],
                            ps_b[g % 2][0:64, 0:512]))
                    s_scan.inc(v.tensor_tensor_scan(
                        states_b[0:64, :], ab_sb[0:64, :], u_sb[0:64, :],
                        0.0, ALU.mult, ALU.add))

            with ExitStack() as psc, nc.Block() as blk:
                pe = psc.enter_context
                ps_y = [pe(nc.psum_tensor(_nm(), [128, 512], F32)) for _ in range(3)]
                ps_pad = pe(nc.psum_tensor(_nm(), [128, 512], F32))

                @blk.tensor
                def _(t):
                    s_scan.w(t, 1)
                    s_w2.w(t, 48)
                    for mc in range(DC):
                        for g in range(NG):
                            idx = mc * NG + g
                            if idx >= 3:
                                s_yev.w(t, idx - 2)
                            s_ymm.inc(t.matmul(
                                ps_y[idx % 3][:, 0:512],
                                wc_sb[:, mc * 128:(mc + 1) * 128],
                                states_b[0:64, g * 512:(g + 1) * 512],
                                start=True, stop=True))
                    for _ in range(4):
                        s_ymm.inc(t.matmul(ps_pad[:, 0:512], k.identb[:],
                                           xn[:, 0:512], start=True, stop=True))

                @blk.vector
                def _(v):
                    for mc in range(DC):
                        for g in range(NG):
                            idx = mc * NG + g
                            s_ymm.w(v, idx + 1)
                            sl = slice(mc * T + g * 512, mc * T + g * 512 + 512)
                            s_yev.inc(v.scalar_tensor_tensor(
                                h[:, sl], xn[:, sl], vecs[:, 34 + mc:35 + mc],
                                ps_y[idx % 3][:, 0:512], ALU.mult, ALU.add))

            k.rel(s_w2, s_gb, s_gev, s_u, s_scan, s_ymm, s_yev, s_aexp)
            emit_ln(k, None, hn, gamma_col=16, beta_col=24, vecs=vecs, h_src=h)

            if globals().get("_DBG"):
                dbg = {"d_ab": (ab_sb, F32, [64, T]),
                       "d_gate": (gate_sb, BF16, [64, T]),
                       "d_u": (u_sb, F32, [64, T]),
                       "d_st": (states_b, BF16, [64, T]),
                       "d_h": (h, BF16, [128, DC * T]),
                       "d_hn": (hn, BF16, [128, DC * T]),
                       "d_xn": (xn, BF16, [128, DC * T])}
                (sdb,) = k.sems(1, dma=True)
                outs = {n: nc.dram_tensor(n, shp, dt, kind="ExternalOutput").ap()
                        for n, (_, dt, shp) in dbg.items()}
                with nc.Block() as blk:
                    @blk.gpsimd
                    def _(q):
                        for n, (tile, _, _2) in dbg.items():
                            sdb.inc(q.dma_start(outs[n], tile[:]), 16)
                        sdb.w(q, 16 * len(dbg))
                k.rel(sdb)

        emit_resid_proj(
            k, lambda kc, g: hn[:, kc * T + g * 512: kc * T + g * 512 + 512],
            W["a_wout"], DC, DC, x_hbm, y_hbm, vecs, bias_col=8)
        k.rel(s_v)


def emit_glu(k, sc, psc, w1_hbm, w2_hbm, Mc, Kc, hout, hT, rhs_fn, ng=NG,
             silu_scale_imm=None, s1_col=None, s2_col=None, vecs=None):
    nc = k.nc
    e = sc.enter_context
    wb1 = [e(nc.sbuf_tensor(_nm(), [128, Kc * 128], BF16)) for _ in range(2)]
    wb2 = [e(nc.sbuf_tensor(_nm(), [128, Kc * 128], BF16)) for _ in range(2)]
    sg = e(nc.sbuf_tensor(_nm(), [128, 2 * 512], BF16))
    ug = e(nc.sbuf_tensor(_nm(), [128, 2 * 512], BF16))
    pe = psc.enter_context
    ps_g = [pe(nc.psum_tensor(_nm(), [128, 512], F32)) for _ in range(2)]
    ps_u = [pe(nc.psum_tensor(_nm(), [128, 512], F32)) for _ in range(2)]
    ps_pad = pe(nc.psum_tensor(_nm(), [128, 512], F32))
    (s_w,) = k.sems(1, dma=True)
    s_mm, s_sev, s_hev = k.sems(3)
    ntot = Mc * ng

    with nc.Block() as blk:
        @blk.gpsimd
        def _(q):
            for mc in range(Mc):
                if mc >= 2:
                    s_hev.w(q, (mc - 1) * ng)
                s_w.inc(q.dma_start(wb1[mc % 2][:], w1_hbm[mc]), 16)
                s_w.inc(q.dma_start(wb2[mc % 2][:], w2_hbm[mc]), 16)

        @blk.tensor
        def _(t):
            for mc in range(Mc):
                s_w.w(t, 32 * (mc + 1))
                for g in range(ng):
                    idx = mc * ng + g
                    if idx >= 2:
                        s_hev.w(t, idx - 1)
                    for kc in range(Kc):
                        ins = t.matmul(ps_g[idx % 2][:, 0:512],
                                       wb1[mc % 2][:, kc * 128:(kc + 1) * 128],
                                       rhs_fn(kc, g), start=(kc == 0),
                                       stop=(kc == Kc - 1))
                    s_mm.inc(ins)
                    for kc in range(Kc):
                        ins = t.matmul(ps_u[idx % 2][:, 0:512],
                                       wb2[mc % 2][:, kc * 128:(kc + 1) * 128],
                                       rhs_fn(kc, g), start=(kc == 0),
                                       stop=(kc == Kc - 1))
                    s_mm.inc(ins)
            for _ in range(4):
                s_mm.inc(t.matmul(ps_pad[:, 0:512], k.identb[:],
                                  wb1[0][:, 0:512], start=True, stop=True))

        @blk.scalar
        def _(scl):
            for mc in range(Mc):
                for g in range(ng):
                    idx = mc * ng + g
                    s_mm.w(scl, 2 * idx + 2)
                    if idx >= 2:
                        s_hev.w(scl, idx - 1)
                    so = slice((idx % 2) * 512, (idx % 2) * 512 + 512)
                    if silu_scale_imm is not None:
                        scl.activation(sg[:, so], ps_g[idx % 2][:, 0:512],
                                       AF.Silu, scale=float(silu_scale_imm))
                        i2 = scl.copy(ug[:, so], ps_u[idx % 2][:, 0:512])
                    else:
                        scl.activation(sg[:, so], ps_g[idx % 2][:, 0:512],
                                       AF.Silu,
                                       scale=vecs[:, s1_col + mc:s1_col + mc + 1])
                        i2 = scl.activation(ug[:, so], ps_u[idx % 2][:, 0:512],
                                            AF.Copy,
                                            scale=vecs[:, s2_col + mc:s2_col + mc + 1])
                    s_sev.inc(i2)

        @blk.vector
        def _(v):
            for mc in range(Mc):
                for g in range(ng):
                    idx = mc * ng + g
                    s_sev.w(v, idx + 1)
                    so = slice((idx % 2) * 512, (idx % 2) * 512 + 512)
                    s_hev.inc(v.tensor_mul(
                        hout[:, mc * hT + g * 512: mc * hT + g * 512 + 512],
                        sg[:, so], ug[:, so]))
    k.rel(s_w, s_mm, s_sev, s_hev)


def emit_l0ffn(k, x_hbm, y_hbm, W):
    nc = k.nc
    with ExitStack() as sc:
        e = sc.enter_context
        vecs = e(nc.sbuf_tensor(_nm(), [128, 1], F32))
        xn = e(nc.sbuf_tensor(_nm(), [128, DC * T], BF16))
        emit_ln(k, x_hbm, xn)
        h = e(nc.sbuf_tensor(_nm(), [128, H0C * T], BF16))
        with ExitStack() as sc2, ExitStack() as psc:
            emit_glu(k, sc2, psc, W["f_wgate"], W["f_wup"], H0C, DC, h, T,
                     lambda kc, g: xn[:, kc * T + g * 512: kc * T + g * 512 + 512],
                     silu_scale_imm=QS)
        emit_resid_proj(
            k, lambda kc, g: h[:, kc * T + g * 512: kc * T + g * 512 + 512],
            W["f_wdown"], DC, H0C, x_hbm, y_hbm, vecs, scale_imm=QS * QS)


def emit_l1attn(k, x_hbm, y_hbm, W):
    nc = k.nc
    with ExitStack() as sc:
        e = sc.enter_context
        vecs = e(nc.sbuf_tensor(_nm(), [128, 28], F32))
        z_tm = e(nc.sbuf_tensor(_nm(), [128, 16 * 1024], BF16))
        zsum = e(nc.sbuf_tensor(_nm(), [128, 256], F32))
        bv_sb = e(nc.sbuf_tensor(_nm(), [1, 1024], BF16))
        (s_v,) = k.sems(1, dma=True)
        with nc.Block() as blk:
            @blk.gpsimd
            def _(q):
                s_v.inc(q.dma_start(vecs[:, 0:8], W["c_bq"]), 16)
                s_v.inc(q.dma_start(vecs[:, 8:16], W["c_bk"]), 16)
                s_v.inc(q.dma_start(vecs[:, 16:17], W["c_wqg"]), 16)
                s_v.inc(q.dma_start(vecs[:, 17:18], W["c_wkg"]), 16)
                s_v.inc(q.dma_start(bv_sb[:], W["c_bv"]), 16)
                s_v.inc(q.dma_start(vecs[:, 20:28], W["c_bo"]), 16)

        with ExitStack() as scq:
            eq = scq.enter_context
            xn = eq(nc.sbuf_tensor(_nm(), [128, DC * T], BF16))
            emit_ln(k, x_hbm, xn)
            q_fm = eq(nc.sbuf_tensor(_nm(), [128, DC * T], BF16))
            k_fm = eq(nc.sbuf_tensor(_nm(), [128, DC * T], BF16))
            v_tm = eq(nc.sbuf_tensor(_nm(), [128, 16 * 1024], BF16))

            if True:
                with ExitStack() as sc3, ExitStack() as psc:
                    e3 = sc3.enter_context
                    wsb = [e3(nc.sbuf_tensor(_nm(), [128, DC * 1024], BF16))
                           for _ in range(2)]
                    pe = psc.enter_context
                    ps = [pe(nc.psum_tensor(_nm(), [128, 512], F32))
                          for _ in range(3)]
                    ps_pad = pe(nc.psum_tensor(_nm(), [128, 512], F32))
                    (s_w,) = k.sems(1, dma=True)
                    s_mm, s_ev = k.sems(2)
                    NSEC = DC * NG

                    with nc.Block() as blk:
                        @blk.gpsimd
                        def _(q):
                            for sec, wname in enumerate(("c_wq", "c_wk", "c_wv")):
                                if sec >= 2:
                                    s_ev.w(q, (sec - 1) * NSEC)
                                s_w.inc(_wload(q, wsb[sec % 2], W[wname]), 16)

                        @blk.tensor
                        def _(t):
                            s_v.w(t, 96)
                            idx = 0
                            for sec in range(3):
                                s_w.w(t, 16 * (sec + 1))
                                for a in range(DC if sec < 2 else 16):
                                    for b in range(NG if sec < 2 else 2):
                                        if idx >= 3:
                                            s_ev.w(t, idx - 2)
                                        p = ps[idx % 3]
                                        if sec < 2:
                                            for kc in range(DC):
                                                ins = t.matmul(
                                                    p[:, 0:512],
                                                    wsb[sec % 2][:, a * 1024 + kc * 128:
                                                                 a * 1024 + kc * 128 + 128],
                                                    xn[:, kc * T + b * 512:
                                                       kc * T + b * 512 + 512],
                                                    start=(kc == 0),
                                                    stop=(kc == DC - 1))
                                        else:
                                            for kc in range(DC):
                                                t.matmul(
                                                    p[:, 0:512],
                                                    xn[:, kc * T + a * 128:
                                                       kc * T + a * 128 + 128],
                                                    wsb[sec % 2][:, kc * 1024 + b * 512:
                                                                 kc * 1024 + b * 512 + 512],
                                                    start=(kc == 0), stop=False)
                                            ins = t.matmul(
                                                p[:, 0:512], k.onesr[:],
                                                bv_sb[:, b * 512: b * 512 + 512],
                                                start=False, stop=True)
                                        s_mm.inc(ins)
                                        idx += 1
                            for _ in range(4):
                                s_mm.inc(t.matmul(ps_pad[:, 0:512], k.identb[:],
                                                  xn[:, 0:512], start=True,
                                                  stop=True))

                        @blk.scalar
                        def _(scl):
                            total = 3 * NSEC
                            idx = 0
                            for sec in range(2):
                                bc = 0 if sec == 0 else 8
                                dst = q_fm if sec == 0 else k_fm
                                for mc in range(DC):
                                    for g in range(NG):
                                        s_mm.w(scl, idx + 1)
                                        s_ev.inc(scl.activation(
                                            dst[:, mc * T + g * 512:
                                                mc * T + g * 512 + 512],
                                            ps[idx % 3][:, 0:512], AF.Identity,
                                            bias=vecs[:, bc + mc:bc + mc + 1]))
                                        idx += 1

                        @blk.vector
                        def _(v):
                            total = 3 * NSEC
                            idx = 2 * NSEC
                            for tt in range(16):
                                for hf in range(2):
                                    s_mm.w(v, idx + 1)
                                    s_ev.inc(v.tensor_copy(
                                        v_tm[:, tt * 1024 + hf * 512:
                                             tt * 1024 + hf * 512 + 512],
                                        ps[idx % 3][:, 0:512]))
                                    idx += 1
                    k.rel(s_w, s_mm, s_ev)

            with ExitStack() as sc3, ExitStack() as psc:
                e3 = sc3.enter_context
                qsq = e3(nc.sbuf_tensor(_nm(), [128, 2 * T], BF16))
                rows_f = e3(nc.sbuf_tensor(_nm(), [32, T], F32))
                rows_s = e3(nc.sbuf_tensor(_nm(), [32, T], F32))
                rows_b = e3(nc.sbuf_tensor(_nm(), [32, T], BF16))
                pe = psc.enter_context
                ps_st = pe(nc.psum_tensor(_nm(), [32, T], F32))
                ps_bc = [pe(nc.psum_tensor(_nm(), [128, 512], F32))
                         for _ in range(3)]
                s_sq, s_stmm, s_row, s_bcmm, s_app = k.sems(5)

                with nc.Block() as blk:
                    @blk.scalar
                    def _(scl):
                        for c in range(16):
                            src = q_fm if c < DC else k_fm
                            cc = c % DC
                            if c >= 2:
                                s_stmm.w(scl, 4 * (c - 1))
                            s_sq.inc(scl.activation(
                                qsq[:, (c % 2) * T:(c % 2) * T + T],
                                src[:, cc * T:(cc + 1) * T], AF.Square))
                        s_row.w(scl, 1)
                        scl.activation(rows_s[:], rows_f[:], AF.Ln)
                        s_row.inc(scl.activation(rows_b[:], rows_s[:], AF.Exp,
                                                 scale=-0.5), 2)

                    @blk.tensor
                    def _(t):
                        for c in range(16):
                            s_sq.w(t, c + 1)
                            for g in range(NG):
                                ins = t.matmul(
                                    ps_st[:, g * 512:(g + 1) * 512],
                                    k.mask16[:, c * 32:(c + 1) * 32],
                                    qsq[:, (c % 2) * T + g * 512:
                                        (c % 2) * T + g * 512 + 512],
                                    start=(c == 0), stop=(c == 15))
                                s_stmm.inc(ins)
                        for _ in range(4):
                            s_stmm.inc(t.matmul(ps_bc[0][:, 0:512], k.identb[:],
                                                qsq[:, 0:512], start=True,
                                                stop=True))
                        s_row.w(t, 3)
                        for c in range(16):
                            for g in range(NG):
                                idx = c * NG + g
                                if idx >= 3:
                                    s_app.w(t, idx - 2)
                                s_bcmm.inc(t.matmul(
                                    ps_bc[idx % 3][:, 0:512],
                                    k.sel16[:, c * 128:(c + 1) * 128],
                                    rows_b[:, g * 512:(g + 1) * 512],
                                    start=True, stop=True))
                        for _ in range(4):
                            s_bcmm.inc(t.matmul(ps_bc[0][:, 0:512] if False
                                                else ps_st[:, 0:512],
                                                k.mask16[:, 0:32],
                                                qsq[:, 0:512], start=True,
                                                stop=True))

                    @blk.vector
                    def _(v):
                        s_stmm.w(v, 4 * 16 + 4)
                        s_row.inc(v.tensor_scalar(rows_f[:], ps_st[:, 0:T],
                                                  1.0 / 64.0, float(RMS_EPS),
                                                  ALU.mult, ALU.add))
                        for c in range(16):
                            src = q_fm if c < DC else k_fm
                            cc = c % DC
                            wcol = 16 if c < DC else 17
                            for g in range(NG):
                                idx = c * NG + g
                                s_bcmm.w(v, idx + 1)
                                sl = slice(cc * T + g * 512, cc * T + g * 512 + 512)
                                s_app.inc(v.scalar_tensor_tensor(
                                    src[:, sl], src[:, sl],
                                    vecs[:, wcol:wcol + 1],
                                    ps_bc[idx % 3][:, 0:512],
                                    ALU.mult, ALU.mult))
                k.rel(s_sq, s_stmm, s_row, s_bcmm, s_app)

            with ExitStack() as sc4, ExitStack() as psc:
                e4 = sc4.enter_context
                exp_sb = [e4(nc.sbuf_tensor(_nm(), [128, T], BF16)) for _ in range(2)]
                expT_sb = [e4(nc.sbuf_tensor(_nm(), [128, T], BF16)) for _ in range(2)]
                nmt4 = e4(nc.sbuf_tensor(_nm(), [128, 8], F32))
                nmt4b = e4(nc.sbuf_tensor(_nm(), [128, 8], F32))
                nm = [e4(nc.sbuf_tensor(_nm(), [128, 1], F32)) for _ in range(2)]
                kz = [e4(nc.sbuf_tensor(_nm(), [128, T], BF16)) for _ in range(2)]
                pe = psc.enter_context
                ps_sc = [pe(nc.psum_tensor(_nm(), [128, 512], F32))
                         for _ in range(6)]
                ps_av = [pe(nc.psum_tensor(_nm(), [128, 65], F32)) for _ in range(2)]
                NI = 256
                s_sc, s_mx4, s_nmb, s_nm, s_exp, s_trev0, s_trev1, \
                    s_avmm, s_zev, s_kz = k.sems(10)
                s_trev = {0: s_trev0, 1: s_trev1}
                exp3 = [e[:].rearrange("p (t x) -> p t x", x=128)
                        for e in exp_sb]
                expT3 = [e[:].rearrange("p (t x) -> p t x", x=128)
                         for e in expT_sb]

                with nc.Block() as blk:
                    @blk.sync
                    def _(q):
                        for J in range(NI):
                            for kh in (0, 1):
                                s_exp.w(q, 4 * J + 2 * (kh + 1))
                                if J >= 2:
                                    s_avmm.w(q, 2 * (J - 2) + 2)
                                s_trev[kh].inc(q.dma_start(
                                    expT3[J % 2][:, kh * 8:kh * 8 + 8, :],
                                    exp3[J % 2][:, kh * 8:kh * 8 + 8, :],
                                    transpose=True), 16)

                    @blk.tensor
                    def _(t):
                        for I in range(NI + 2):
                            if I < NI:
                                h_ = I >> 4
                                qt = I & 15
                                ch = h_ >> 1
                                qsl = slice(ch * T + qt * 128, ch * T + qt * 128 + 128)
                                s_kz.w(t, h_ + 1)
                                for gq in range(4):
                                    g = 4 * I + gq
                                    if g >= 6:
                                        s_exp.w(t, g - 5)
                                    s_sc.inc(t.matmul(
                                        ps_sc[g % 6][:, 0:512],
                                        q_fm[:, qsl],
                                        kz[h_ % 2][:, gq * 512:gq * 512 + 512],
                                        start=True, stop=True))
                            Jv = I - 2
                            if 0 <= Jv < NI:
                                h2 = Jv >> 4
                                for kh in (0, 1):
                                    s_trev[kh].w(t, 16 * (Jv + 1))
                                    if kh == 0 and Jv >= 2:
                                        s_zev.w(t, Jv - 1)
                                    for j in range(8):
                                        jt = kh * 8 + j
                                        t.matmul(
                                            ps_av[Jv % 2][:, 0:64],
                                            expT_sb[Jv % 2][:, jt * 128:jt * 128 + 128],
                                            v_tm[:, jt * 1024 + h2 * 64:
                                                 jt * 1024 + h2 * 64 + 64],
                                            start=(kh == 0 and j == 0), stop=False)
                                        ins = t.matmul(
                                            ps_av[Jv % 2][:, 64:65],
                                            expT_sb[Jv % 2][:, jt * 128:jt * 128 + 128],
                                            k.onesc[:],
                                            start=False, stop=(kh == 1 and j == 7))
                                    s_avmm.inc(ins)
                        for _ in range(4):
                            s_avmm.inc(t.matmul(ps_sc[0][:, 0:512], k.identb[:],
                                                q_fm[:, 0:512], start=True,
                                                stop=True))

                    @blk.vector
                    def _(v):
                        def kz_build(H):
                            if H >= 2:
                                s_sc.w(v, 64 * (H - 1))
                            po = (H & 1) * 64
                            chh = H >> 1
                            s_kz.inc(v.tensor_copy(
                                kz[H % 2][po:po + 64, :],
                                k_fm[po:po + 64, chh * T:chh * T + T]))

                        v.memset(kz[0][64:128, 0:1024], 0.0)
                        v.memset(kz[0][64:128, 1024:2048], 0.0)
                        v.memset(kz[1][0:64, 0:1024], 0.0)
                        v.memset(kz[1][0:64, 1024:2048], 0.0)
                        kz_build(0)
                        kz_build(1)
                        for I in range(NI + 2):
                            if (I < NI and I % 16 == 13
                                    and 2 <= (I >> 4) + 1 < 16):
                                kz_build((I >> 4) + 1)
                            if I < NI:
                                for gq in range(4):
                                    s_sc.w(v, 4 * I + gq + 1)
                                    s_mx4.inc(v.reduce_max(
                                        nmt4[:, (I % 2) * 4 + gq:
                                             (I % 2) * 4 + gq + 1],
                                        ps_sc[(4 * I + gq) % 6][:, 0:512],
                                        axis=AX.X, negate=True))
                            if I < NI:
                                for ip in range(6):
                                    a0, b0 = (0, 64) if ip % 2 == 0 else (64, 0)
                                    v.tensor_copy(k.pad_sb[:, a0:a0 + 64],
                                                  k.pad_sb[:, b0:b0 + 64])
                                if I >= 2:
                                    s_exp.w(v, 4 * (I - 2) + 4)
                                v.tensor_copy(
                                    nmt4b[:, (I % 2) * 4:(I % 2) * 4 + 4],
                                    nmt4[:, (I % 2) * 4:(I % 2) * 4 + 4])
                                s_nm.inc(v.tensor_reduce(
                                    nm[I % 2][:],
                                    nmt4b[:, (I % 2) * 4:(I % 2) * 4 + 4],
                                    axis=AX.X, op=ALU.min))
                            Jv = I - 2
                            if 0 <= Jv < NI:
                                h2 = Jv >> 4
                                qt2 = Jv & 15
                                s_avmm.w(v, 2 * Jv + 2)
                                v.tensor_copy(
                                    z_tm[:, qt2 * 1024 + h2 * 64:
                                         qt2 * 1024 + h2 * 64 + 64],
                                    ps_av[Jv % 2][:, 0:64])
                                s_zev.inc(v.tensor_copy(
                                    zsum[:, Jv:Jv + 1],
                                    ps_av[Jv % 2][:, 64:65]))

                    @blk.scalar
                    def _(scl):
                        for I in range(NI + 1):
                            if I < NI:
                                if I >= 2:
                                    s_trev0.w(scl, 16 * (I - 1))
                                    s_trev1.w(scl, 16 * (I - 1))
                                s_nm.w(scl, I + 1)
                                for gq in range(4):
                                    s_exp.inc(scl.activation(
                                        exp_sb[I % 2][:, gq * 512:gq * 512 + 512],
                                        ps_sc[(4 * I + gq) % 6][:, 0:512],
                                        AF.Exp, bias=nm[I % 2][:]))
                k.rel(s_sc, s_mx4, s_nmb, s_nm, s_exp, s_trev0, s_trev1,
                      s_avmm, s_zev, s_kz)

            if globals().get("_DBGC"):
                dbg = {"dc_xn": (xn, BF16, [128, DC * T]),
                       "dc_q": (q_fm, BF16, [128, DC * T]),
                       "dc_k": (k_fm, BF16, [128, DC * T]),
                       "dc_v": (v_tm, BF16, [128, 16 * 1024]),
                       "dc_z": (z_tm, BF16, [128, 16 * 1024]),
                       "dc_zs": (zsum, F32, [128, 256])}
                (sdb,) = k.sems(1, dma=True)
                outs = {n: nc.dram_tensor(n, shp, dt, kind="ExternalOutput").ap()
                        for n, (_, dt, shp) in dbg.items()}
                with nc.Block() as blk:
                    @blk.gpsimd
                    def _(q):
                        for n, (tile, _, _2) in dbg.items():
                            sdb.inc(q.dma_start(outs[n], tile[:]), 16)
                        sdb.w(q, 16 * len(dbg))
                k.rel(sdb)

        with ExitStack() as sc5:
            e5 = sc5.enter_context
            z_fm = e5(nc.sbuf_tensor(_nm(), [128, DC * T], BF16))
            rc_all = e5(nc.sbuf_tensor(_nm(), [128, 256], F32))
            s_nz, = k.sems(1)
            with nc.Block() as blk:
                @blk.vector
                def _(v):
                    v.reciprocal(rc_all[:], zsum[:])
                    for ip in range(10):
                        a0, b0 = (0, 64) if ip % 2 == 0 else (64, 0)
                        v.tensor_copy(k.pad_sb[:, a0:a0 + 64],
                                      k.pad_sb[:, b0:b0 + 64])
                    for qt in range(16):
                        for h_ in range(16):
                            col = h_ * 16 + qt
                            sl = slice(qt * 1024 + h_ * 64,
                                       qt * 1024 + h_ * 64 + 64)
                            ins = v.tensor_scalar_mul(z_tm[:, sl], z_tm[:, sl],
                                                      rc_all[:, col:col + 1])
                        s_nz.inc(ins)
            with ExitStack() as psc, nc.Block() as blk:
                pe = psc.enter_context
                ps_z = [pe(nc.psum_tensor(_nm(), [128, 1024], BF16))
                        for _ in range(2)]
                ps_pad = pe(nc.psum_tensor(_nm(), [128, 512], F32))
                s_zt, s_ze = k.sems(2)

                @blk.tensor
                def _(t):
                    idx = 0
                    for c in range(DC):
                        for b in range(2):
                            s_nz.w(t, 8 * (b + 1))
                            if idx >= 2:
                                s_ze.w(t, idx - 1)
                            for j in range(8):
                                tt = b * 8 + j
                                ins = t.transpose(
                                    ps_z[idx % 2][:, j * 128:(j + 1) * 128],
                                    z_tm[:, tt * 1024 + c * 128:
                                         tt * 1024 + c * 128 + 128],
                                    k.identb[:])
                            s_zt.inc(ins)
                            idx += 1
                    for _ in range(4):
                        s_zt.inc(t.matmul(ps_pad[:, 0:512], k.identb[:],
                                          z_tm[:, 0:512], start=True, stop=True))

                @blk.vector
                def _(v):
                    idx = 0
                    for c in range(DC):
                        for b in range(2):
                            s_zt.w(v, idx + 1)
                            s_ze.inc(v.tensor_copy(
                                z_fm[:, c * T + b * 1024: c * T + b * 1024 + 1024],
                                ps_z[idx % 2][:, 0:1024]))
                            idx += 1
                k.rel(s_nz, s_zt, s_ze)

            if globals().get("_DBGC"):
                (sdb2,) = k.sems(1, dma=True)
                zfm_out = nc.dram_tensor("dc_zfm", [128, DC * T], BF16,
                                         kind="ExternalOutput").ap()
                with nc.Block() as blk:
                    @blk.gpsimd
                    def _(q):
                        sdb2.inc(q.dma_start(zfm_out, z_fm[:]), 16)
                        sdb2.w(q, 16)
                k.rel(sdb2)

            emit_resid_proj(
                k, lambda kc, g: z_fm[:, kc * T + g * 512: kc * T + g * 512 + 512],
                W["c_wo"], DC, DC, x_hbm, y_hbm, vecs, bias_col=20)
        k.rel(s_v)


def emit_l1ffn(k, x_hbm, out_hbm, W):
    nc = k.nc
    with ExitStack() as sc:
        e = sc.enter_context
        vecs = e(nc.sbuf_tensor(_nm(), [128, 136], F32))
        xn = e(nc.sbuf_tensor(_nm(), [128, DC * T], BF16))
        (s_v,) = k.sems(1, dma=True)
        with nc.Block() as blk:
            @blk.gpsimd
            def _(q):
                s_v.inc(q.dma_start(vecs[:, 0:32], W["d_s1"]), 16)
                s_v.inc(q.dma_start(vecs[:, 32:64], W["d_s2"]), 16)
                s_v.inc(q.dma_start(vecs[:, 64:96], W["d_h1s"]), 16)
                s_v.inc(q.dma_start(vecs[:, 96:128], W["d_h2s"]), 16)
                s_v.inc(q.dma_start(vecs[:, 128:136], W["d_w3s"]), 16)

        emit_ln(k, x_hbm, xn)
        with nc.Block() as blk:
            @blk.vector
            def _(v):
                s_v.w(v, 80)

        HT = 1024
        for hf in range(2):
            with ExitStack() as sch:
                eh = sch.enter_context
                a1 = eh(nc.sbuf_tensor(_nm(), [128, H1C * HT], BF16))

                def rhs_xn(kc, g, hf=hf):
                    o = kc * T + hf * HT + g * 512
                    return xn[:, o:o + 512]

                with ExitStack() as sc2, ExitStack() as psc:
                    emit_glu(k, sc2, psc, W["d_w1"], W["d_w2"], H1C, DC, a1, HT,
                             rhs_xn, ng=2, s1_col=0, s2_col=32, vecs=vecs)

                with ExitStack() as sca:
                    a2 = sca.enter_context(
                        nc.sbuf_tensor(_nm(), [128, H1C * HT], BF16))
                    for (wname, scol, src, dst) in (("d_h1", 64, a1, a2),
                                                    ("d_h2", 96, a2, a1)):
                        with ExitStack() as sc2, ExitStack() as psc:
                            pr = Proj(k, sc2, psc, W[wname], H1C, H1C,
                                      (lambda s: lambda kc, g:
                                       s[:, kc * HT + g * 512: kc * HT + g * 512 + 512])(src),
                                      ng=2)
                            ps_pad = psc.enter_context(
                                nc.psum_tensor(_nm(), [128, 512], F32))
                            with nc.Block() as blk:
                                @blk.gpsimd
                                def _(q):
                                    pr.emit_dma(q)

                                @blk.tensor
                                def _(t):
                                    pr.emit_pe(t, ps_pad)

                                @blk.scalar
                                def _(scl):
                                    def ev(eng, mc, g, ps, dst=dst, scol=scol):
                                        return eng.activation(
                                            dst[:, mc * HT + g * 512:
                                                mc * HT + g * 512 + 512],
                                            ps, AF.Gelu,
                                            scale=vecs[:, scol + mc:scol + mc + 1])
                                    pr.emit_evac(scl, ev)
                            pr.done()

                emit_final(k, a1, x_hbm, out_hbm, W["d_w3"], vecs, hf)


def emit_final(k, a3, x_hbm, out_hbm, w3_hbm, vecs, hf):
    nc = k.nc
    HT = 1024
    with ExitStack() as sc, ExitStack() as psc:
        e = sc.enter_context
        xres = e(nc.sbuf_tensor(_nm(), [128, 2 * HT], F32))
        outfm = e(nc.sbuf_tensor(_nm(), [128, 3 * 512], F32))
        stage = e(nc.sbuf_tensor(_nm(), [128, 8 * 1024], F32))
        wbuf = [e(nc.sbuf_tensor(_nm(), [128, H1C * 128], BF16)) for _ in range(2)]
        pe = psc.enter_context
        ps = [pe(nc.psum_tensor(_nm(), [128, 512], F32)) for _ in range(3)]
        ps_tr = [pe(nc.psum_tensor(_nm(), [128, 512], F32)) for _ in range(2)]
        ps_pad = pe(nc.psum_tensor(_nm(), [128, 512], F32))
        s_w, s_res, s_st = k.sems(3, dma=True)
        s_mm, s_oev, s_trmm, s_trev = k.sems(4)
        NTOT = DC * 2
        stage3 = stage[:].rearrange("p (tb x) -> p tb x", x=1024)

        with nc.Block() as blk:
            @blk.gpsimd
            def _(q):
                for mc in range(DC):
                    if mc >= 2:
                        s_oev.w(q, (mc - 1) * 2)
                    s_w.inc(q.dma_start(wbuf[mc % 2][:], w3_hbm[mc]), 16)

            @blk.sync
            def _(q):
                for mc in range(DC):
                    if mc >= 2:
                        s_oev.w(q, (mc - 1) * 2)
                    s_res.inc(q.dma_start(
                        xres[:, (mc % 2) * HT:(mc % 2) * HT + HT],
                        x_hbm[:, mc * T + hf * HT: mc * T + hf * HT + HT]), 16)
                for tb in range(8):
                    s_trev.w(q, 15 + tb // 4)
                    s_st.inc(q.dma_start(
                        out_hbm[hf * HT + tb * 128: hf * HT + tb * 128 + 128, :],
                        stage3[:, tb, :]), 16)
                s_st.w(q, 16 * 8)

            @blk.tensor
            def _(t):
                def tr_group(j):
                    s_oev.w(t, j + 1)
                    if j >= 2:
                        s_trev.w(t, j - 1)
                    for jj in range(4):
                        ins = t.transpose(
                            ps_tr[j % 2][:, jj * 128:(jj + 1) * 128],
                            outfm[:, (j % 3) * 512 + jj * 128:
                                  (j % 3) * 512 + jj * 128 + 128],
                            k.identf[:])
                    s_trmm.inc(ins)

                for mc in range(DC):
                    s_w.w(t, 16 * (mc + 1))
                    for g in range(2):
                        idx = mc * 2 + g
                        if idx >= 3:
                            s_oev.w(t, idx - 2)
                        for kc in range(H1C):
                            ins = t.matmul(
                                ps[idx % 3][:, 0:512],
                                wbuf[mc % 2][:, kc * 128:(kc + 1) * 128],
                                a3[:, kc * HT + g * 512: kc * HT + g * 512 + 512],
                                start=(kc == 0), stop=(kc == H1C - 1))
                        s_mm.inc(ins)
                        if idx - 2 >= 0:
                            tr_group(idx - 2)
                for _ in range(4):
                    s_mm.inc(t.matmul(ps_pad[:, 0:512], k.identb[:],
                                      a3[:, 0:512], start=True, stop=True))
                for j in (NTOT - 2, NTOT - 1):
                    tr_group(j)
                for _ in range(2):
                    s_trmm.inc(t.transpose(ps_pad[:, 0:128], k.identf[:],
                                           k.identf[:]))

            @blk.vector
            def _(v):
                def trev(j):
                    mc2, g2 = j // 2, j % 2
                    s_trmm.w(v, j + 1)
                    dst = stage3[:, g2 * 4:g2 * 4 + 4, mc2 * 128:mc2 * 128 + 128]
                    src = ps_tr[j % 2][:, 0:512].rearrange(
                        "p (a b) -> p a b", b=128)
                    s_trev.inc(v.tensor_copy(dst, src))

                for mc in range(DC):
                    for g in range(2):
                        idx = mc * 2 + g
                        s_mm.w(v, idx + 1)
                        s_res.w(v, 16 * (mc + 1))
                        if idx >= 3:
                            s_trmm.w(v, idx - 2)
                        s_oev.inc(v.scalar_tensor_tensor(
                            outfm[:, (idx % 3) * 512:(idx % 3) * 512 + 512],
                            ps[idx % 3][:, 0:512],
                            vecs[:, 128 + mc:128 + mc + 1],
                            xres[:, (mc % 2) * HT + g * 512:
                                 (mc % 2) * HT + g * 512 + 512],
                            ALU.mult, ALU.add))
                        if idx - 2 >= 0:
                            trev(idx - 2)
                for j in (NTOT - 2, NTOT - 1):
                    trev(j)
        k.rel(s_w, s_mm, s_res, s_oev, s_trmm, s_trev, s_st)


def _host_prep(inputs):
    f32 = np.float32
    g = lambda n: np.asarray(inputs[n])
    Wqkv = g("L1_attn_Wqkv").astype(f32)
    bqkv = g("L1_attn_bqkv").astype(f32)
    prep = {
        "a_wg": _prep_w(g("L0_attn_Wg")), "a_wbx": _prep_w(g("L0_attn_WB")),
        "a_wc": _prep_w(g("L0_attn_WC")), "a_wout": _prep_w(g("L0_attn_Wout")),
        "a_D": _prep_pc(g("L0_attn_D")), "a_bout": _prep_pc(g("L0_attn_bout")),
        "a_gamma": _prep_pc(g("L0_attn_gamma")),
        "a_beta": _prep_pc(g("L0_attn_beta")),
        "a_Alog": _prep_pc(g("L0_attn_A_log")), "a_bg": _prep_pc(g("L0_attn_bg")),
        "f_wgate": _prep_w(g("L0_ffn_w_gate")), "f_wup": _prep_w(g("L0_ffn_w_up")),
        "f_wdown": _prep_w(g("L0_ffn_w_down")),
        "c_wq": _prep_w(Wqkv[0:1024]), "c_wk": _prep_w(Wqkv[1024:2048]),
        "c_wv": _prep_w(Wqkv[2048:3072]),
        "c_bq": _prep_pc(bqkv[0:1024]), "c_bk": _prep_pc(bqkv[1024:2048]),
        "c_bv": np.ascontiguousarray(bqkv[2048:3072].reshape(1, 1024)),
        "c_wqg": np.ascontiguousarray(
            np.concatenate([g("L1_attn_wq")] * 2).reshape(128, 1).astype(f32)),
        "c_wkg": np.ascontiguousarray(
            np.concatenate([g("L1_attn_wk")] * 2).reshape(128, 1).astype(f32)),
        "c_wo": _prep_w(g("L1_attn_Wo")), "c_bo": _prep_pc(g("L1_attn_bo")),
        "d_w1": _prep_w(g("L1_ffn_w1_w")), "d_w2": _prep_w(g("L1_ffn_w2_w")),
        "d_h1": _prep_w(g("L1_ffn_h1_w")), "d_h2": _prep_w(g("L1_ffn_h2_w")),
        "d_w3": _prep_w(g("L1_ffn_w3_w")),
        "d_s1": _prep_pc(g("L1_ffn_w1_s")), "d_s2": _prep_pc(g("L1_ffn_w2_s")),
        "d_h1s": _prep_pc(g("L1_ffn_h1_s")), "d_h2s": _prep_pc(g("L1_ffn_h2_s")),
        "d_w3s": _prep_pc(g("L1_ffn_w3_s")),
    }
    return prep


_BLOCK_WNAMES = {
    "A": ["a_wg", "a_wbx", "a_wc", "a_wout", "a_D", "a_bout", "a_gamma",
          "a_beta", "a_Alog", "a_bg"],
    "B": ["f_wgate", "f_wup", "f_wdown"],
    "C": ["c_wq", "c_wk", "c_wv", "c_bq", "c_bk", "c_bv", "c_wqg", "c_wkg",
          "c_wo", "c_bo"],
    "D": ["d_w1", "d_w2", "d_h1", "d_h2", "d_w3", "d_s1", "d_s2", "d_h1s",
          "d_h2s", "d_w3s"],
}

_EMITTERS = {"A": emit_l0attn, "B": emit_l0ffn, "C": emit_l1attn,
             "D": emit_l1ffn}


def build_block(which, prep):
    nc = bass.Bass("TRN2", target_bir_lowering=False,
                  detect_race_conditions=False)
    dt_of = lambda a: {np.dtype(np.float32): F32, np.dtype(np.int32): I32}[a.dtype]
    W = {n: nc.dram_tensor(n, list(prep[n].shape), dt_of(prep[n]),
                           kind="ExternalInput").ap()
         for n in _BLOCK_WNAMES[which]}
    x_in = nc.dram_tensor("x_in", [128, DC * T], F32, kind="ExternalInput").ap()
    if which == "D":
        x_out = nc.dram_tensor("x_out", [T, D], F32, kind="ExternalOutput").ap()
    else:
        x_out = nc.dram_tensor("x_out", [128, DC * T], F32,
                               kind="ExternalOutput").ap()
    with ExitStack() as stack:
        k = K(nc, stack)
        k.load_consts()
        _EMITTERS[which](k, x_in, x_out, W)
    return nc


def build_fused(prep):
    nc = bass.Bass("TRN2", target_bir_lowering=False,
                  detect_race_conditions=False)
    dt_of = lambda a: {np.dtype(np.float32): F32, np.dtype(np.int32): I32}[a.dtype]
    W = {n: nc.dram_tensor(n, list(prep[n].shape), dt_of(prep[n]),
                           kind="ExternalInput").ap()
         for n in prep}
    x_in = nc.dram_tensor("x_in", [128, DC * T], F32, kind="ExternalInput").ap()
    xA = nc.dram_tensor("xA", [128, DC * T], F32).ap()
    xB = nc.dram_tensor("xB", [128, DC * T], F32).ap()
    xC = nc.dram_tensor("xC", [128, DC * T], F32).ap()
    out = nc.dram_tensor("x_out", [T, D], F32, kind="ExternalOutput").ap()
    with ExitStack() as stack:
        k = K(nc, stack)
        k.load_consts()
        emit_l0attn(k, x_in, xA, W)
        emit_l0ffn(k, xA, xB, W)
        emit_l1attn(k, xB, xC, W)
        emit_l1ffn(k, xC, out, W)
    return nc


FUSED = True


def kernel(**inputs):
    prep = _host_prep(inputs)
    x0 = np.asarray(inputs["L0_attn_x"]).astype(np.float32)
    xs = [_fm(x0[b]) for b in range(8)]
    if FUSED:
        nc = build_fused(prep)
        in_maps = [{**prep, "x_in": xs[b]} for b in range(8)]
        res = run_bass_kernel_spmd(nc, in_maps, core_ids=list(range(8)))
        return np.stack([res.results[b]["x_out"] for b in range(8)], axis=0)
    for which in ("A", "B", "C", "D"):
        nc = build_block(which, prep)
        wmap = {n: prep[n] for n in _BLOCK_WNAMES[which]}
        in_maps = [{**wmap, "x_in": xs[b]} for b in range(8)]
        res = run_bass_kernel_spmd(nc, in_maps, core_ids=list(range(8)))
        xs = [res.results[b]["x_out"] for b in range(8)]
    return np.stack(xs, axis=0)
